# version 01
# baseline (speedup 1.0000x reference)
import contextlib
import numpy as np
import concourse.bass as bass
import concourse.mybir as mybir
from concourse.bass_utils import run_bass_kernel_spmd

F32 = mybir.dt.float32
BF16 = mybir.dt.bfloat16
FP8 = mybir.dt.float8e4
ALU = mybir.AluOpType
ACT = mybir.ActivationFunctionType

NCORES = 8
T = 2048
D = 2048
NH = 8
DV = 256
QK = 1024
NCH = 16
FFI = 6144
NTT = 16
LN_EPS = 1e-5
MH_EPS = 1e-6
QSCALE = 128.0 ** -0.5
NDT = FFI // 128


def build(taps=(), upto=None):
    nc = bass.Bass("TRN2", target_bir_lowering=False, detect_race_conditions=False)

    din = lambda name, shape, dtype=F32: nc.dram_tensor(name, list(shape), dtype, kind="ExternalInput").ap()
    dout = lambda name, shape, dtype=F32: nc.dram_tensor(name, list(shape), dtype, kind="ExternalOutput").ap()
    dint = lambda name, shape, dtype=F32: nc.dram_tensor(name, list(shape), dtype, kind="Internal").ap()

    x_in = din("x", (T, D))
    Wq = din("Wq", (QK, D)); Wk = din("Wk", (QK, D))
    Wv = din("Wv", (D, D)); Wog = din("Wog", (D, D)); Wout_i = din("Wout", (D, D))
    Wig = din("Wig", (NH, D)); Wfg = din("Wfg", (NH, D))
    big = din("big", (NH, 1)); bfg = din("bfg", (NH, 1))
    gamma_i = din("gamma", (NH, DV))
    gate_w = din("gate_w", (D, FFI)); up_w = din("up_w", (D, FFI)); down_w = din("down_w", (FFI, D))
    gate_si = din("gate_s", (1, FFI)); up_si = din("up_s", (1, FFI)); down_si = din("down_s", (1, D))
    ident_in = din("ident", (128, 128), BF16)
    id8_in = din("id8", (8, 8))
    tri_in = din("tri", (128, 128))
    umask_in = din("umask", (NH, NCORES - 1))

    y_out = dout("y", (T, D))

    q_sp = dint("q_sp", (NH, 128, T), BF16)
    v_sp = dint("v_sp", (T, D), BF16)
    sog_sp = dint("sog_sp", (T, D), BF16)
    hout_sp = dint("hout_sp", (2, 128, 16, T // 2), BF16)
    xmid_sp = dint("xmid_sp", (T, D))
    xlt_sp = dint("xlt_sp", (2, 128, 16, T // 2), BF16)
    down8_sp = dint("down8_sp", (NDT, 128, D), FP8)
    rows_sp = dint("rows_sp", (NH, 48))
    rows_sp2 = dint("rows_sp2", (NH, 16))
    cc_in = dint("cc_in", (128, 2064))
    cc_out = dint("cc_out", (NCORES * 128, 2064))

    tapd = {}
    for tn, shape, dt in (
        ("t_xln", (128, 16, T), BF16), ("t_kT", (128, NH, T), BF16),
        ("t_i", (NH, T), F32), ("t_flog", (NH, T), F32), ("t_bt", (NH, T), F32),
        ("t_sum", (128, 2064), F32), ("t_ent", (128, NH * (DV + 1)), F32),
        ("t_min", (NH, 4), F32), ("t_mt", (NH, T), F32), ("t_gs8", (NH, 512), F32), ("t_bca", (128, 128), F32), ("t_ho", (256, 16384), BF16),
    ):
        if tn in taps:
            tapd[tn] = dout(tn, shape, dt)

    if True:
        sbg = lambda name, shape, dtype: nc.alloc_sbuf_tensor(name, list(shape), dtype)

        ident = sbg("s_ident", (128, 128), BF16)
        id8 = sbg("s_id8", (8, 8), F32)
        tri = sbg("s_tri", (128, 128), F32)
        onescol = sbg("onescol", (128, 1), BF16)
        ones1r = sbg("ones1r", (1, 128), F32)
        umask = sbg("s_umask", (NH, NCORES - 1), F32)
        bias15 = sbg("bias15", (NH, 2), F32)
        rowpack = sbg("rowpack", (NH, 48), F32)
        MIN8 = sbg("MIN8", (NH, 4), F32)
        rec4 = sbg("rec4", (128, 16), F32)
        stat6 = sbg("stat6", (128, 32), F32)
        srows = sbg("srows", (128, 4), F32)
        row16 = sbg("row16", (1, 16), F32)
        wkb = sbg("wkb", (128, 2, 128), BF16)
        wstb = sbg("wstb", (128, 2, 128), BF16)
        IH = sbg("IH", (NH, T), F32)
        FT = sbg("FT", (NH, T), F32)
        dq_stage = sbg("dq_stage", (128, 2, D), FP8)
        m8s = sbg("m8s", (NH, 8), F32)
        cbias = sbg("cbias", (128, 2), F32)
        nc.const_aps.register(F32, LN_EPS, cbias[:, 0:1])
        nc.const_aps.register(F32, MH_EPS, cbias[:, 1:2])

        rem = nc.sbuf_bytes_remaining
        AR = rem - 2048
        arena_h = nc.alloc_sbuf_tensor("arena", [128, AR], mybir.dt.uint8)
        ar_base = nc.lookup_mloc(arena_h).addr
        made = {}

        def at(name, shape, dtype, off):
            assert name not in made
            per = int(np.prod(shape[1:])) * mybir.dt.size(dtype)
            assert off + per <= AR, (name, off, per, AR)
            h = nc.alloc_sbuf_tensor_at(name, list(shape), dtype, offset=ar_base + off)
            made[name] = h
            return h

        def lay(*sizes):
            offs = []
            cur = None
            for s in sizes:
                if cur is None:
                    cur = s
                    continue
                offs.append(cur)
                cur += (s + 31) // 32 * 32
            offs.append(cur)
            return offs

        O_kT = 0
        AFTER_KT = 32768
        (O_xa, O_xlnst, O_xlnT, O_WTq, O_WTg, O_WTv, O_pst, A_END) = lay(
            AFTER_KT, 16384, 8192, 65536, 8192, 256, 32768, 2048)
        O_wst = O_xa
        (O_CM, O_C0A, O_RC, O_GALL, O_EMALL, O_BCA, O_GS8, O_FLOG, O_BT, O_MT,
         O_vch, O_sogch, O_qTs, O_gb, O_houtst, O_houtTc, O_gbuf, O_vaug,
         O_hraw, O_hmb, O_hgb, B_END) = lay(
            AFTER_KT, 8224, 4112, 512, 512, 512, 1536, 2048, 8192, 8192, 8192,
            8192, 8192, 32768, 8192, 4096, 4096, 8256, 4112, 2048, 2048, 2048)
        (O_wstW, O_houtW, O_WTw, O_xaW, O_ystW, O_x2st, O_x2T, O_xbW, W_END) = lay(
            0, 8192, 32768, 32768, 16384, 4096, 8192, 32768, 16384)
        (O_fuse, O_x2Tf, O_g8, O_u8, O_gub, O_uub, O_gsus, F1_END) = lay(
            0, 98304, 32768, 16384, 16384, 4096, 4096, 512)
        (O_dn8, O_dsb, O_xaF, O_ystF, F2_END) = lay(
            O_g8, 4096, 8192, 16384, 4096)
        assert max(A_END, B_END, W_END, F1_END, F2_END) <= AR, (A_END, B_END, W_END, F1_END, F2_END, AR)

        kT = at("kT", (128, NH, T), BF16, O_kT)
        xa = at("xa", (128, 2, D), F32, O_xa)
        wst = at("wst", (128, 2, D), BF16, O_wst)
        xlnst = at("xlnst", (128, 2, D), BF16, O_xlnst)
        xlnT = at("xlnT", (128, 16, T), BF16, O_xlnT)
        WTq4 = at("WTq", (128, 2 * 16 * 128), BF16, O_WTq)
        WTq = lambda b, a: WTq4[:, (b * 16 + a) * 128:(b * 16 + a + 1) * 128]
        WTg = at("WTg", (128, 16, 8), BF16, O_WTg)
        WTv4 = at("WTv", (128, 2 * 16 * 512), BF16, O_WTv)
        WTv = lambda b, a: WTv4[:, (b * 16 + a) * 512:(b * 16 + a + 1) * 512]
        pstage = at("pstage", (128, 2, 512), BF16, O_pst)

        CM = at("CM", (128, NH, DV + 1), F32, O_CM)
        C0A = at("C0A", (128, NH, DV + 1), BF16, O_C0A)
        RC = at("RC", (128, NCH, NH), F32, O_RC)
        GALL = at("GALL", (128, NCH, NH), F32, O_GALL)
        EMALL = at("EMALL", (128, NCH, NH), F32, O_EMALL)
        BCA = at("BCA", (128, 384), F32, O_BCA)
        GS8 = at("GS8", (8, 512), F32, O_GS8)
        FLOG = at("FLOG", (8, T), F32, O_FLOG)
        BT = at("BT", (8, T), F32, O_BT)
        MT = at("MT", (8, T), F32, O_MT)
        vch = at("vch", (128, 2, D), BF16, O_vch)
        sogch = at("sogch", (128, 2, D), BF16, O_sogch)
        qTs = at("qTs", (128, NH, T), BF16, O_qTs)
        gamma_bc = at("gamma_bc", (128, D), F32, O_gb)
        houtst = at("houtst", (128, D), BF16, O_houtst)
        houtTc = at("houtTc", (128, 16, 128), BF16, O_houtTc)
        gbuf = at("gbuf", (128, 2064), F32, O_gbuf)
        vaug = at("vaug", (128, NH, DV + 1), BF16, O_vaug)
        h_raw = at("h_raw", (128, NH, DV), F32, O_gbuf)
        hmb = at("hmb", (128, 2, DV), F32, O_hmb)
        hgb = at("hgb", (128, 2, DV), F32, O_hgb)

        wstW = at("wstW", (128, 2, D), BF16, O_wstW)
        houtW = at("houtW", (128, 16, T // 2), BF16, O_houtW)
        WTw4 = at("WTw", (128, 2 * 16 * 512), BF16, O_WTw)
        WTw = lambda b, a: WTw4[:, (b * 16 + a) * 512:(b * 16 + a + 1) * 512]
        xaW = at("xaW", (128, 2, D), F32, O_xaW)
        ystW = at("ystW", (128, 2, 512), F32, O_ystW)
        x2st = at("x2st", (128, 2, D), BF16, O_x2st)
        x2T = at("x2T", (128, 16, T // 2), BF16, O_x2T)
        xbW = at("xbW", (128, 2, D), F32, O_xbW)

        fuse = at("fuse", (128, NDT, T // 2), BF16, O_fuse)
        x2Tf = at("x2Tf", (128, 16, T // 2), BF16, O_x2Tf)
        g8r = at("g8", (128, 2 * 16 * 512), FP8, O_g8)
        g8 = lambda b: g8r.ap()[:, b * 8192:(b + 1) * 8192].rearrange("p (a n) -> p a n", a=16)
        g8f = lambda b, a: g8r[:, (b * 16 + a) * 512:(b * 16 + a + 1) * 512]
        u8r = at("u8", (128, 2 * 16 * 512), FP8, O_u8)
        u8 = lambda b: u8r.ap()[:, b * 8192:(b + 1) * 8192].rearrange("p (a n) -> p a n", a=16)
        u8f = lambda b, a: u8r[:, (b * 16 + a) * 512:(b * 16 + a + 1) * 512]
        gub = at("gub", (128, 2, 512), F32, O_gub)
        uub = at("uub", (128, 2, 512), F32, O_uub)
        gsus = at("gsus", (128, 128), F32, O_gsus)
        dn8 = at("dn8", (128, 2, D), FP8, O_dn8)
        ds_bc = at("ds_bc", (128, D), F32, O_dsb)
        xaF = at("xaF", (128, 2, D), F32, O_xaF)
        ystF = at("ystF", (128, 2, 512), F32, O_ystF)

        PB = [nc.alloc_psum_tensor(f"pb{k}", [128, 512], F32) for k in range(8)]

        def pbf(k, n=512):
            return PB[k][:, 0:n]

        def pbb(k, n=128):
            return PB[k].ap()[:, 0:(n + 1) // 2].bitcast(BF16)

        def pb8(k):
            return PB[k][0:8, 0:512]

        sems = {}
        ctr = {}
        for s in ('mm', 'vec', 'scl', 'gps', 'cc', 'bar'):
            sems[s] = nc.alloc_semaphore(name=s)
            ctr[s] = 0
        dstreams = {}

        def dget(name, depth=2):
            if name not in dstreams:
                ss = [nc.alloc_semaphore(name=f"{name}{k}") for k in range(depth)]
                dstreams[name] = dict(sems=ss, depth=depth, items=0)
            return dstreams[name]

        def dinc(instr, name, i, depth=2):
            st = dget(name, depth)
            instr.then_inc(st['sems'][i % st['depth']], 16)
            st['items'] = max(st['items'], i + 1)

        def dwait(eng, name, i, depth=2):
            if i < 0:
                return
            st = dget(name, depth)
            eng.wait_ge(st['sems'][i % st['depth']], 16 * (i // st['depth'] + 1))

        def dwait_total(eng, name):
            if name not in dstreams:
                return
            st = dstreams[name]
            n = st['items']
            d = st['depth']
            for k in range(d):
                cnt = (n - k + d - 1) // d
                if cnt > 0:
                    eng.wait_ge(st['sems'][k], 16 * cnt)

        def dbase(name, depth=2):
            return dget(name, depth)['items']
        blk_cm = nc.Block()
        B = blk_cm.__enter__()

        def reserve(s, n):
            b = ctr[s]
            ctr[s] += n
            return b

        def barrier():
            ctr['bar'] += 5
            tgt = ctr['bar']
            for secf in (B.sync, B.scalar, B.tensor, B.vector, B.gpsimd):
                def _f(e, tgt=tgt):
                    e.sem_inc(sems['bar'], 1)
                    e.wait_ge(sems['bar'], tgt)
                secf(_f)

        c0v = reserve('vec', 1)

        def p0_g(g):
            dinc(g.dma_start(out=ident[:, :], in_=ident_in), 'p0c', 0, 1)
            dinc(g.dma_start(out=id8[:, :], in_=id8_in), 'p0c', 1, 1)
            dinc(g.dma_start(out=tri[:, :], in_=tri_in), 'p0c', 2, 1)
            dinc(g.dma_start(out=umask[:, :], in_=umask_in), 'p0c', 3, 1)
            dinc(g.dma_start(out=bias15[:, 0:1], in_=big), 'p0c', 4, 1)
            dinc(g.dma_start(out=bias15[:, 1:2], in_=bfg), 'p0c', 5, 1)
            g.memset(onescol[:, :], 1.0)
            g.memset(ones1r[:, :], 1.0)
            g.memset(srows[:, :], 0.0)
            g.memset(MIN8[:, :], 0.0)
            g.memset(cbias[:, 0:1], LN_EPS)
            g.memset(cbias[:, 1:2], MH_EPS)
        B.gpsimd(p0_g)

        def p0_v(v):
            dwait_total(v, 'p0c')
            v.tensor_scalar_mul(bias15[:, :], bias15[:, :], 1.0 / 15.0).then_inc(sems['vec'], 1)
        B.vector(p0_v)
        P0_B15 = c0v + 1

        def p0_dq_g(g):
            for nt in range(NDT):
                dwait(g, 'dqst', nt - 2)
                dinc(g.dma_start(out=dq_stage[:, nt % 2, :],
                                 in_=down_w[nt * 128:(nt + 1) * 128, :]), 'dqld', nt)
        B.gpsimd(p0_dq_g)

        def p0_dq_s(s):
            for nt in range(NDT):
                dwait(s, 'dqld', nt)
                dinc(s.dma_start(out=down8_sp[nt, :, :],
                                 in_=dq_stage[:, nt % 2, :]), 'dqst', nt)
        B.scalar(p0_dq_s)
        DQ_DONE = None

        _stop = upto if isinstance(upto, str) else ''
        _order = ['A1', 'Aq', 'Aqk', 'Agates', 'Avog', 'A']
        _later = ['B1', 'C', 'B2', 'W']
        def _skip(stage):
            return _stop in _order and _order.index(_stop) < _order.index(stage)

        SE = {'tensor': 'mm', 'vector': 'vec', 'scalar': 'scl', 'gpsimd': 'gps'}

        def run_pipe(plan_fn):
            P = {'tensor': [], 'vector': [], 'scalar': [], 'gpsimd': [], 'sync': []}

            def add(eng, key, fn):
                P[eng].append((key, fn))
            plan_fn(add)
            bases = {eng: reserve(SE[eng], len(lst)) for eng, lst in P.items() if eng != 'sync'}
            posmap = {eng: {k: i + 1 for i, (k, _) in enumerate(lst)}
                      for eng, lst in P.items()}

            def pos(eng, key):
                return bases[eng] + posmap[eng][key]
            for eng, lst in P.items():
                if not lst:
                    continue
                def sec(e, lst=lst, eng=eng):
                    for k, fn in lst:
                        r = fn(e, pos)
                        if eng != 'sync':
                            if r is not None:
                                r.then_inc(sems[SE[eng]], 1)
                            else:
                                e.sem_inc(sems[SE[eng]], 1)
                getattr(B, eng)(sec)
            return pos

        def pbt(slot, a):
            bank = 2 * slot + a // 8
            off = (a % 8) * 64
            return PB[bank].ap()[:, off:off + 64].bitcast(BF16)

        def pbt8(a):
            off = a * 4
            return PB[0].ap()[:, off:off + 4].bitcast(BF16)

        def a1_plan(add):
            def dma_x(tt):
                def fn(e, pos):
                    if tt >= 2:
                        e.wait_ge(sems['vec'], pos('vector', ('n', tt - 2)))
                    dinc(e.dma_start(out=xa[:, tt % 2, :],
                                     in_=x_in[tt * 128:(tt + 1) * 128, :]), 'xld', tt)
                    return None
                return fn

            def statsp(tt):
                def fn(v, pos):
                    dwait(v, 'xld', tt)
                    last = None
                    for q4 in range(4):
                        last = v.bn_stats(stat6[:, 6 * q4:6 * q4 + 6], xa[:, tt % 2, q4 * 512:(q4 + 1) * 512])
                    return last
                return fn

            def stats(tt):
                def fn(v, pos):
                    v.wait_ge(sems['vec'], pos('vector', ('sp', tt)))
                    return v.bn_aggr(stat6[:, 24:26], stat6[:, 0:24])
                return fn

            def rstd(tt):
                def fn(s, pos):
                    s.wait_ge(sems['vec'], pos('vector', ('s', tt)))
                    s.activation(srows[:, 1:2], stat6[:, 25:26], ACT.Ln, bias=LN_EPS)
                    return s.activation(srows[:, 0:1], srows[:, 1:2], ACT.Exp, scale=-0.5)
                return fn

            def norm(tt):
                def fn(v, pos):
                    v.wait_ge(sems['scl'], pos('scalar', ('r', tt)))
                    if tt >= 2:
                        v.wait_ge(sems['mm'], pos('tensor', ('t', tt - 2, 15)))
                    return v.tensor_scalar(xlnst[:, tt % 2, :], xa[:, tt % 2, :],
                                           stat6[:, 24:25], srows[:, 0:1],
                                           ALU.subtract, ALU.mult)
                return fn

            def tr(tt, a):
                def fn(t, pos):
                    if tt == 0 and a == 0:
                        dwait_total(t, 'p0c')
                    if a == 0:
                        t.wait_ge(sems['vec'], pos('vector', ('n', tt)))
                        if tt >= 2:
                            t.wait_ge(sems['vec'], pos('vector', ('e', tt - 2, 15)))
                    return t.transpose(pbt(tt % 2, a), xlnst[:, tt % 2, a * 128:(a + 1) * 128],
                                       ident[:, :])
                return fn

            def ev(tt, a):
                def fn(v, pos):
                    v.wait_ge(sems['mm'], pos('tensor', ('t', tt, 7 if a < 8 else 15)))
                    return v.tensor_copy(xlnT[:, a, tt * 128:(tt + 1) * 128], pbt(tt % 2, a))
                return fn

            for tt in range(NTT):
                add('sync', ('x', tt), dma_x(tt))
                add('vector', ('sp', tt), statsp(tt))
                add('vector', ('s', tt), stats(tt))
                add('scalar', ('r', tt), rstd(tt))
                add('vector', ('n', tt), norm(tt))
                if tt >= 1:
                    for a in range(16):
                        add('vector', ('e', tt - 1, a), ev(tt - 1, a))
                for a in range(16):
                    add('tensor', ('t', tt, a), tr(tt, a))
            for a in range(16):
                add('vector', ('e', NTT - 1, a), ev(NTT - 1, a))
        run_pipe(a1_plan)

        barrier()
        if "t_xln" in tapd:
            def tapf(s):
                dinc(s.dma_start(out=tapd["t_xln"], in_=xlnT[:, :, :]), 'tap', dbase('tap', 1), 1)
                dwait_total(s, 'tap')
            B.sync(tapf)

        import os as _os2
        _nogrp = False
        _notr = False
        _nowev = False

        def emit_fm(W_ap, nblk, lname, gev_fn, gev_extra_wait=None):
            def plan(add):
                def dma_w(ofb):
                    def fn(e, pos):
                        if ofb >= 2:
                            e.wait_ge(sems['mm'], pos('tensor', ('t', ofb - 1, 15)))
                        dinc(e.dma_start(out=wst[:, ofb % 2, :],
                                         in_=W_ap[ofb * 128:(ofb + 1) * 128, :]), lname, ofb)
                        return None
                    return fn

                def tr(ofb, a):
                    def fn(t, pos):
                        if a == 0:
                            dwait(t, lname, ofb)
                            if ofb >= 2 and not _notr:
                                t.wait_ge(sems['vec'], pos('vector', ('w', ofb - 2, 15)))
                        if _notr:
                            return t.nop()
                        return t.transpose(pbt(ofb % 2, a), wst[:, ofb % 2, a * 128:(a + 1) * 128],
                                           ident[:, :])
                    return fn

                def wev(ofb, a):
                    def fn(v, pos):
                        v.wait_ge(sems['mm'], pos('tensor', ('t', ofb, 7 if a < 8 else 15)))
                        if _notr or _nowev:
                            return v.engine_nop()
                        return v.tensor_copy(WTq(ofb % 2, a), pbt(ofb % 2, a))
                    return fn

                def grp(ofb, tb):
                    def fn(t, pos):
                        if tb == 0:
                            t.wait_ge(sems['vec'], pos('vector', ('w', ofb, 15)))
                        if ofb >= 1:
                            t.wait_ge(sems['vec'], pos('vector', ('g', ofb - 1, tb)))
                        last = None
                        for a in range(16):
                            last = t.matmul(pbf(4 + tb), WTq(ofb % 2, a),
                                            xlnT[:, a, tb * 512:(tb + 1) * 512],
                                            start=(a == 0), stop=(a == 15))
                        return last
                    return fn

                def gev(ofb, tb):
                    def fn(v, pos):
                        v.wait_ge(sems['mm'], pos('tensor', ('g', ofb, tb)))
                        return gev_fn(v, pos, ofb, tb, pbf(4 + tb))
                    return fn

                for b in range(nblk + 1):
                    if b < nblk:
                        add('gpsimd', ('l', b), dma_w(b))
                    if b >= 1 and not _nogrp:
                        for tb in range(4):
                            add('tensor', ('g', b - 1, tb), grp(b - 1, tb))
                    if b < nblk:
                        for a in range(16):
                            add('tensor', ('t', b, a), tr(b, a))
                    if b >= 1:
                        for a in range(16):
                            add('vector', ('w', b - 1, a), wev(b - 1, a))
                        if not _nogrp:
                            for tb in range(4):
                                add('vector', ('g', b - 1, tb), gev(b - 1, tb))
            return run_pipe(plan)

        import os as _os
        _noqst = False
        if not _skip('Aq'):
            def q_gev(v, pos, ofb, tb, pp):
                idx = ofb * 4 + tb
                if not _noqst:
                    dwait(v, 'qst', idx - 2)
                return v.tensor_scalar_mul(pstage[:, idx % 2, :], pp, QSCALE)
            q_pos = emit_fm(Wq, QK // 128, 'wqld', q_gev)

            if not _noqst:
                def q_ssec(s):
                    for ofb in range(QK // 128):
                        for tb in range(4):
                            idx = ofb * 4 + tb
                            s.wait_ge(sems['vec'], q_pos('vector', ('g', ofb, tb)))
                            dinc(s.dma_start(out=q_sp[ofb, :, tb * 512:(tb + 1) * 512],
                                             in_=pstage[:, idx % 2, :]), 'qst', idx)
                B.scalar(q_ssec)
            barrier()

            pass
        if not _skip('Aqk'):
            def k_gev(v, pos, ofb, tb, pp):
                return v.tensor_copy(kT[:, ofb, tb * 512:(tb + 1) * 512], pp)
            emit_fm(Wk, QK // 128, 'wkld', k_gev)
            barrier()

        for gi, (Wg_ap, bcol, dstT) in enumerate((((Wig, 0, IH), (Wfg, 1, FT)) if not _skip('Agates') else ())):
            def g_plan(add, Wg_ap=Wg_ap, bcol=bcol, dstT=dstT, gi=gi):
                def dma_w(e, pos):
                    dinc(e.dma_start(out=wst[0:8, gi, :], in_=Wg_ap), 'gtld', gi)
                    return None
                add('gpsimd', 'l', dma_w)

                def tr(a):
                    def fn(t, pos):
                        if a == 0:
                            dwait(t, 'gtld', gi)
                        return t.transpose(pbt8(a), wst[0:8, gi, a * 128:(a + 1) * 128],
                                           ident[0:8, 0:8])
                    return fn

                def wev(a):
                    def fn(v, pos):
                        v.wait_ge(sems['mm'], pos('tensor', ('t', 15)))
                        return v.tensor_copy(WTg[:, a, :], pbt8(a))
                    return fn

                def grp(tb):
                    def fn(t, pos):
                        if tb == 0:
                            t.wait_ge(sems['vec'], pos('vector', ('w', 15)))
                        if gi >= 1 or tb >= 0:
                            pass
                        t.wait_ge(sems['scl'], pos('scalar', ('a', tb)) - 1)
                        last = None
                        for a in range(16):
                            last = t.matmul(pb8(4 + tb), WTg[:, a, :],
                                            xlnT[:, a, tb * 512:(tb + 1) * 512],
                                            start=(a == 0), stop=(a == 15))
                        return last
                    return fn

                def act(tb):
                    def fn(s, pos):
                        if tb == 0:
                            s.wait_ge(sems['vec'], P0_B15)
                        s.wait_ge(sems['mm'], pos('tensor', ('g', tb)))
                        return s.activation(dstT[:, tb * 512:(tb + 1) * 512], pb8(4 + tb), ACT.Tanh,
                                            bias=bias15[:, bcol:bcol + 1], scale=1.0 / 15.0)
                    return fn

                for a in range(16):
                    add('tensor', ('t', a), tr(a))
                for a in range(16):
                    add('vector', ('w', a), wev(a))
                for tb in range(4):
                    add('tensor', ('g', tb), grp(tb))
                    add('scalar', ('a', tb), act(tb))
            run_pipe(g_plan)
            barrier()

        for W_ap, kind in (((Wv, 'v'), (Wog, 'og')) if not _skip('Avog') else ()):
            lname = 'wvld' if kind == 'v' else 'wogld'
            sname = 'vst' if kind == 'v' else 'ogst'

            def a4_plan(add, W_ap=W_ap, kind=kind, lname=lname, sname=sname):
                def dma_w(it):
                    def fn(e, pos):
                        if it >= 2:
                            e.wait_ge(sems['mm'], pos('tensor', ('t', it - 1, 15)))
                        dinc(e.dma_start(
                            out=wst[:, it % 2, :],
                            in_=W_ap[it * 128:(it + 1) * 128, :]), lname, it)
                        return None
                    return fn

                def tr(it, a):
                    def fn(t, pos):
                        if a == 0:
                            dwait(t, lname, it)
                            if it >= 2:
                                t.wait_ge(sems['vec'], pos('vector', ('w', it - 2, 15)))
                        return t.transpose(pbt(it % 2, a), wst[:, it % 2, a * 128:(a + 1) * 128],
                                           ident[:, :])
                    return fn

                def wev(it, a):
                    def fn(v, pos):
                        v.wait_ge(sems['mm'], pos('tensor', ('t', it, 7 if a < 8 else 15)))
                        vb, rr = divmod(it, 4)
                        return v.tensor_copy(WTv(vb % 2, a)[:, rr * 128:(rr + 1) * 128],
                                             pbt(it % 2, a))
                    return fn

                def grp(vb, tt):
                    def fn(t, pos):
                        if tt == 0:
                            t.wait_ge(sems['vec'], pos('vector', ('w', vb * 4 + 3, 15)))
                        idx = vb * 16 + tt
                        if idx >= 4:
                            if kind == 'v':
                                t.wait_ge(sems['vec'], pos('vector', ('g', (idx - 4) // 16, (idx - 4) % 16)))
                            else:
                                t.wait_ge(sems['scl'], pos('scalar', ('g', (idx - 4) // 16, (idx - 4) % 16)))
                        last = None
                        for a in range(16):
                            last = t.matmul(pbf(4 + idx % 4), xlnT[:, a, tt * 128:(tt + 1) * 128],
                                            WTv(vb % 2, a),
                                            start=(a == 0), stop=(a == 15))
                        return last
                    return fn

                def gev(vb, tt):
                    idx = vb * 16 + tt
                    if kind == 'v':
                        def fn(v, pos):
                            v.wait_ge(sems['mm'], pos('tensor', ('g', vb, tt)))
                            dwait(v, sname, idx - 2)
                            return v.tensor_copy(pstage[:, idx % 2, :], pbf(4 + idx % 4))
                        return ('vector', fn)
                    else:
                        def fn(s, pos):
                            s.wait_ge(sems['mm'], pos('tensor', ('g', vb, tt)))
                            dwait(s, sname, idx - 2)
                            return s.activation(pstage[:, idx % 2, :], pbf(4 + idx % 4), ACT.Sigmoid)
                        return ('scalar', fn)

                for vb in range(5):
                    if vb >= 1:
                        for tt in range(NTT):
                            add('tensor', ('g', vb - 1, tt), grp(vb - 1, tt))
                    if vb < 4:
                        for rr in range(4):
                            it = vb * 4 + rr
                            for a in range(16):
                                add('tensor', ('t', it, a), tr(it, a))
                    if vb >= 1:
                        for rr in range(4):
                            it = (vb - 1) * 4 + rr
                            for a in range(16):
                                add('vector', ('w', it, a), wev(it, a))
                    if vb >= 1:
                        for tt in range(NTT):
                            eng, fn = gev(vb - 1, tt)
                            add(eng, ('g', vb - 1, tt), fn)
                for it in range(16):
                    add('gpsimd', ('l', it), dma_w(it))
            a4_pos = run_pipe(a4_plan)

            if kind == 'v':
                def v_store(s, a4_pos=a4_pos):
                    for vb in range(4):
                        for tt in range(NTT):
                            idx = vb * 16 + tt
                            s.wait_ge(sems['vec'], a4_pos('vector', ('g', vb, tt)))
                            dinc(s.dma_start(
                                out=v_sp[tt * 128:(tt + 1) * 128, vb * 512:(vb + 1) * 512],
                                in_=pstage[:, idx % 2, :]), 'vst', idx)
                B.scalar(v_store)
                barrier()
            else:
                def og_store(s, a4_pos=a4_pos):
                    for vb in range(4):
                        for tt in range(NTT):
                            idx = vb * 16 + tt
                            s.wait_ge(sems['scl'], a4_pos('scalar', ('g', vb, tt)))
                            dinc(s.dma_start(
                                out=sog_sp[tt * 128:(tt + 1) * 128, vb * 512:(vb + 1) * 512],
                                in_=pstage[:, idx % 2, :]), 'ogst', idx)
                B.sync(og_store)
            barrier()
        if "t_i" in tapd:
            def tapf(s):
                dinc(s.dma_start(out=tapd["t_i"], in_=IH[:, :]), 'tap', dbase('tap', 1), 1)
                dwait_total(s, 'tap')
            B.sync(tapf)
        if "t_kT" in tapd:
            def tapf(s):
                dinc(s.dma_start(out=tapd["t_kT"], in_=kT[:, :, :]), 'tap', dbase('tap', 1), 1)
                dwait_total(s, 'tap')
            B.sync(tapf)

        if _stop in _order:
            def fin_a(s):
                for nm in list(dstreams):
                    dwait_total(s, nm)
            B.sync(fin_a)
        else:
                        emit_rest(locals())
        blk_cm.__exit__(None, None, None)
    return nc


import ml_dtypes

def make_in_maps(full):
    x = np.asarray(full["L0_attn_x"], np.float32).reshape(NCORES * T, D)
    ident = np.eye(128, dtype=ml_dtypes.bfloat16)
    id8 = np.eye(8, dtype=np.float32)
    tri = (np.arange(128)[:, None] <= np.arange(128)[None, :]).astype(np.float32)
    base = dict(
        Wq=np.asarray(full["L0_attn_Wq"], np.float32),
        Wk=np.asarray(full["L0_attn_Wk"], np.float32),
        Wv=np.asarray(full["L0_attn_Wv"], np.float32),
        Wog=np.asarray(full["L0_attn_Wog"], np.float32),
        Wout=np.asarray(full["L0_attn_Wout"], np.float32),
        Wig=np.asarray(full["L0_attn_Wig"], np.float32),
        Wfg=np.asarray(full["L0_attn_Wfg"], np.float32),
        big=np.asarray(full["L0_attn_big"], np.float32).reshape(NH, 1),
        bfg=np.asarray(full["L0_attn_bfg"], np.float32).reshape(NH, 1),
        gamma=np.asarray(full["L0_attn_gamma"], np.float32),
        gate_w=np.asarray(full["L0_ffn_gate_w"], np.float32),
        up_w=np.asarray(full["L0_ffn_up_w"], np.float32),
        down_w=np.asarray(full["L0_ffn_down_w"], np.float32),
        gate_s=np.asarray(full["L0_ffn_gate_scale"], np.float32).reshape(1, FFI),
        up_s=np.asarray(full["L0_ffn_up_scale"], np.float32).reshape(1, FFI),
        down_s=np.asarray(full["L0_ffn_down_scale"], np.float32).reshape(1, D),
        ident=ident, id8=id8, tri=tri,
    )
    in_maps = []
    for core in range(NCORES):
        m = dict(base)
        m["x"] = np.ascontiguousarray(x[core * T:(core + 1) * T])
        m["umask"] = np.broadcast_to(
            (np.arange(NCORES - 1)[None, :] < core).astype(np.float32), (NH, NCORES - 1)).copy()
        in_maps.append(m)
    return in_maps


"""Phases B1 (pass-1), C (exchange), B2 (pass-2), W (Wout), L2 (xmid LN), F (FFN)."""

F32 = mybir.dt.float32
BF16 = mybir.dt.bfloat16
FP8 = mybir.dt.float8e4
ALU = mybir.AluOpType
ACT = mybir.ActivationFunctionType

NCORES = 8
T = 2048
D = 2048
NH = 8
DV = 256
NCH = 16
FFI = 6144
NTT = 16
LN_EPS = 1e-5
MH_EPS = 1e-6
NDT = 48


def emit_rest(L):
    gget = lambda n: L[n]
    upto = L.get('upto')

    def fin_stop(dstreams=None):
        dwait_total = L['dwait_total']
        B = L['B']
        def fin(s):
            for nm in list(L['dstreams']):
                dwait_total(s, nm)
        B.sync(fin)
    nc = gget('nc'); B = gget('B'); sems = gget('sems'); ctr = gget('ctr')
    reserve = gget('reserve'); barrier = gget('barrier')
    pbf = gget('pbf'); pbb = gget('pbb'); PB = gget('PB')
    run_pipe = gget('run_pipe'); pbt = gget('pbt')
    dinc = gget('dinc'); dwait = gget('dwait'); dwait_total = gget('dwait_total'); dbase = gget('dbase')
    tapd = gget('tapd'); P0_B15 = gget('P0_B15'); DQ_DONE = gget('DQ_DONE')
    for _n in ('ident', 'id8', 'tri', 'onescol', 'ones1r', 'umask', 'bias15',
               'rowpack', 'MIN8', 'rec4', 'stat6', 'srows', 'row16', 'wkb', 'wstb',
               'IH', 'FT', 'dq_stage', 'm8s', 'kT', 'CM', 'C0A', 'RC', 'GALL',
               'EMALL', 'BCA', 'GS8', 'FLOG', 'BT', 'MT', 'vch', 'sogch', 'qTs',
               'gamma_bc', 'houtst', 'houtTc', 'gbuf', 'vaug', 'h_raw', 'hmb', 'hgb',
               'wstW', 'houtW', 'WTw', 'xaW', 'xbW', 'ystW', 'x2st', 'x2T',
               'fuse', 'x2Tf', 'g8', 'u8', 'g8f', 'u8f', 'gub', 'uub', 'gsus', 'dn8', 'ds_bc',
               'xaF', 'ystF',
               'x_in', 'Wout_i', 'gamma_i', 'gate_w', 'up_w', 'down_w',
               'gate_si', 'up_si', 'down_si',
               'y_out', 'q_sp', 'v_sp', 'sog_sp', 'hout_sp', 'xmid_sp', 'xlt_sp',
               'down8_sp', 'rows_sp', 'rows_sp2', 'cc_in', 'cc_out'):
        globals()[_n] = gget(_n)

    MTtail = MT.ap().rearrange("h (c l) -> h c l", l=128)

    binit = reserve('gps', 1)

    def b_init(gp):
        gp.memset(CM[:, :, :], 0.0)
        for h in range(NH):
            gp.memset(vaug[:, h, DV:DV + 1], 1.0)
        gp.sem_inc(sems['gps'], 1)
    B.gpsimd(b_init)
    B_INIT = binit + 1

    def emit_prep(is_pass2):
        NVpro = 1 if is_pass2 else 5
        NSpro = 0 if is_pass2 else 2
        NV = NVpro + NCH * 7 + 2
        NS = NSpro + NCH * 3 + 1
        NM = NCH * 3
        ipass = 1 if is_pass2 else 0
        pv = reserve('vec', NV)
        psc = reserve('scl', NS)
        pm = reserve('mm', NM)

        def vsec(v):
            if not is_pass2:
                v.wait_ge(sems['scl'], psc + 1)
                v.memset(FT[:, :], 0.0)
                v.tensor_scalar_add(FLOG[:, :], FLOG[:, :], 1.0).then_inc(sems['vec'], 1)
                v.tensor_scalar_mul(IH[:, :], IH[:, :], 15.0).then_inc(sems['vec'], 1)
                v.wait_ge(sems['scl'], psc + 2)
                v.tensor_scalar_mul(FLOG[:, :], MT[:, :], -1.0).then_inc(sems['vec'], 1)
                v.wait_ge(sems['vec'], pv + 3)
                v.tensor_tensor_scan(BT[:, :], FLOG[:, :], FT[:, :], 0.0,
                                     ALU.add, ALU.add).then_inc(sems['vec'], 1)
                v.tensor_tensor_scan(MT[:, :], FLOG[:, :], IH[:, :], -60.0,
                                     ALU.add, ALU.max).then_inc(sems['vec'], 1)
            else:
                v.tensor_tensor_scan(MT[:, :], FLOG[:, :], IH[:, :], MIN8[:, 0:1],
                                     ALU.add, ALU.max).then_inc(sems['vec'], 1)
            for c in range(NCH):
                sl = slice(c * 128, (c + 1) * 128)
                beta = MIN8[:, 3:4] if c == 0 else BT[:, c * 128 - 1:c * 128]
                v.tensor_scalar(GS8[:, 0:128], BT[:, sl], beta, None,
                                ALU.subtract).then_inc(sems['vec'], 1)
                v.wait_ge(sems['vec'], pv + NVpro + c * 7 + 1)
                v.tensor_tensor(GS8[:, 128:256], IH[:, sl], GS8[:, 0:128],
                                ALU.subtract).then_inc(sems['vec'], 1)
                v.tensor_tensor(GS8[:, 256:384], GS8[:, 0:128], MT[:, sl],
                                ALU.subtract).then_inc(sems['vec'], 1)
                v.wait_ge(sems['mm'], pm + c * 3 + 1)
                v.tensor_copy(RC[:, c, :], pbf(4, 8)).then_inc(sems['vec'], 1)
                v.wait_ge(sems['mm'], pm + c * 3 + 2)
                v.tensor_copy(GALL[:, c, :], pbf(4, 8)).then_inc(sems['vec'], 1)
                v.wait_ge(sems['mm'], pm + c * 3 + 3)
                v.tensor_copy(EMALL[:, c, :], pbf(4, 8)).then_inc(sems['vec'], 1)
                v.tensor_copy(rowpack[:, 16 + c:17 + c], GS8[0:8, 255:256]).then_inc(sems['vec'], 1)
            if is_pass2:
                v.tensor_copy(GS8[:, 384:385], MIN8[:, 0:1])
            else:
                v.memset(GS8[:, 384:385], -60.0)
            v.tensor_copy(GS8[:, 385:400], MTtail[:, 0:15, 127]).then_inc(sems['vec'], 1)
            v.wait_ge(sems['scl'], psc + NS)
            v.wait_ge(sems['vec'], pv + NV - 1)
            v.tensor_tensor(rowpack[:, 32:48], rowpack[:, 0:16], rowpack[:, 16:32],
                            ALU.mult).then_inc(sems['vec'], 1)
        B.vector(vsec)

        def ssec(s):
            if not is_pass2:
                s.wait_ge(sems['vec'], P0_B15)
                s.activation(FLOG[:, :], FT[:, :], ACT.Exp, scale=-15.0).then_inc(sems['scl'], 1)
                s.wait_ge(sems['vec'], pv + 1)
                s.activation(MT[:, :], FLOG[:, :], ACT.Ln).then_inc(sems['scl'], 1)
            for c in range(NCH):
                sl = slice(c * 128, (c + 1) * 128)
                s.wait_ge(sems['vec'], pv + NVpro + c * 7 + 3)
                s.activation(GS8[:, 0:128], GS8[:, 128:256], ACT.Exp).then_inc(sems['scl'], 1)
                s.wait_ge(sems['mm'], pm + c * 3 + 1)
                s.activation(GS8[:, 128:256], GS8[:, 256:384], ACT.Exp).then_inc(sems['scl'], 1)
                s.wait_ge(sems['mm'], pm + c * 3 + 2)
                s.activation(GS8[:, 256:384], MT[:, sl], ACT.Exp, scale=-1.0).then_inc(sems['scl'], 1)
            s.wait_ge(sems['vec'], pv + NV - 1)
            s.activation(rowpack[:, 0:16], GS8[:, 384:400], ACT.Exp).then_inc(sems['scl'], 1)
            s.wait_ge(sems['vec'], pv + NV)
            dinc(s.dma_start(out=rows_sp, in_=rowpack[:, :]), 'rowst', ipass, 1)
        B.scalar(ssec)

        def tsec(t):
            for c in range(NCH):
                t.wait_ge(sems['scl'], psc + NSpro + c * 3 + 1)
                if c > 0:
                    t.wait_ge(sems['vec'], pv + NVpro + (c - 1) * 7 + 6)
                t.matmul(pbf(4, 8), GS8[0:8, 0:128], id8[:, :],
                         start=True, stop=True).then_inc(sems['mm'], 1)
                t.wait_ge(sems['scl'], psc + NSpro + c * 3 + 2)
                t.wait_ge(sems['vec'], pv + NVpro + c * 7 + 4)
                t.matmul(pbf(4, 8), GS8[0:8, 128:256], id8[:, :],
                         start=True, stop=True).then_inc(sems['mm'], 1)
                t.wait_ge(sems['scl'], psc + NSpro + c * 3 + 3)
                t.wait_ge(sems['vec'], pv + NVpro + c * 7 + 5)
                t.matmul(pbf(4, 8), GS8[0:8, 256:384], id8[:, :],
                         start=True, stop=True).then_inc(sems['mm'], 1)
        B.tensor(tsec)

        def gsec(gp):
            dwait(gp, 'rowst', ipass, 1)
            dinc(gp.dma_start(out=BCA[:, :],
                              in_=rows_sp.rearrange("h c -> (h c)")[None, :].partition_broadcast(128)
                              ), 'bcal', ipass, 1)
        B.gpsimd(gsec)
        return ipass

    def bca_em0(h, c): return BCA[:, h * 48 + c: h * 48 + c + 1]
    def bca_gl(h, c): return BCA[:, h * 48 + 16 + c: h * 48 + 17 + c]
    def bca_emg(h, c): return BCA[:, h * 48 + 32 + c: h * 48 + 33 + c]

    bca_p1 = emit_prep(False)
    barrier()
    if "t_flog" in tapd:
        def tapf(s):
            b = dbase('tap', 1)
            dinc(s.dma_start(out=tapd["t_flog"], in_=FLOG[:, :]), 'tap', b, 1)
            dinc(s.dma_start(out=tapd["t_bt"], in_=BT[:, :]), 'tap', b + 1, 1)
            dinc(s.dma_start(out=tapd["t_mt"], in_=MT[:, :]), 'tap', b + 2, 1)
            dwait_total(s, 'tap')
        B.sync(tapf)
        barrier()

    def p1m_trk(i): return 16 * (i // 8) + 2 * (i % 8) + 1
    def p1m_smm(i): return 16 * (i // 8) + 2 * (i % 8) + 2
    def p1v_wk(i):
        c, h = divmod(i, 8)
        return 24 * c + (1 if h == 0 else 3 * h)
    def p1v_va(i):
        c, h = divmod(i, 8)
        return 24 * c + (2 if h == 0 else 3 * h + 1)
    def p1v_cm(i):
        c, h = divmod(i, 8)
        return 24 * c + (24 if h == 7 else 3 * h + 5)

    b1m = reserve('mm', 16 * NCH)
    b1v = reserve('vec', 24 * NCH)

    def b1_sync(s):
        for c in range(NCH):
            if c >= 2:
                s.wait_ge(sems['vec'], b1v + p1v_va(8 * (c - 2) + 7))
            dinc(s.dma_start(out=vch[:, c % 2, :],
                             in_=v_sp[c * 128:(c + 1) * 128, :]), 'vchl', c)
    B.sync(b1_sync)

    def b1_t(t):
        t.wait_ge(sems['gps'], B_INIT)
        for i in range(8 * NCH):
            c, h = divmod(i, 8)
            if i >= 2:
                t.wait_ge(sems['vec'], b1v + p1v_wk(i - 2))
            t.transpose(pbb(i % 2), kT[:, h, c * 128:(c + 1) * 128],
                        ident[:, :]).then_inc(sems['mm'], 1)
            t.wait_ge(sems['vec'], b1v + p1v_va(i))
            if i >= 2:
                t.wait_ge(sems['vec'], b1v + p1v_cm(i - 2))
            t.matmul(pbf(2 + i % 2, DV + 1), wkb[:, i % 2, :], vaug[:, h, :],
                     start=True, stop=True).then_inc(sems['mm'], 1)
    B.tensor(b1_t)

    def b1_v(v):
        dwait(v, 'bcal', bca_p1, 1)
        dwait_total(v, 'p0c')
        v.wait_ge(sems['gps'], B_INIT)
        def cmup(i):
            cc_, hh = divmod(i, 8)
            v.wait_ge(sems['mm'], b1m + p1m_smm(i))
            v.scalar_tensor_tensor(CM[:, hh, :], CM[:, hh, :], bca_emg(hh, cc_),
                                   pbf(2 + i % 2, DV + 1),
                                   ALU.mult, ALU.add).then_inc(sems['vec'], 1)
        for i in range(8 * NCH):
            c, h = divmod(i, 8)
            v.wait_ge(sems['mm'], b1m + p1m_trk(i))
            if i >= 2:
                v.wait_ge(sems['mm'], b1m + p1m_smm(i - 2))
            v.tensor_scalar(wkb[:, i % 2, :], pbb(i % 2), RC[:, c, h:h + 1],
                            bca_gl(h, c), ALU.mult, ALU.mult).then_inc(sems['vec'], 1)
            dwait(v, 'vchl', c)
            if i >= 8:
                v.wait_ge(sems['mm'], b1m + p1m_smm(i - 8))
            v.tensor_copy(vaug[:, h, 0:DV],
                          vch[:, c % 2, h * DV:(h + 1) * DV]).then_inc(sems['vec'], 1)
            if i >= 1:
                cmup(i - 1)
        cmup(8 * NCH - 1)
    B.vector(b1_v)
    barrier()
    if "t_sum" in tapd and upto == 'B1':
        def tapsum(s):
            s.wait_ge(sems['vec'], b1v + 24 * NCH)
            dinc(s.dma_start(out=tapd["t_sum"][:, 0:2056],
                             in_=CM.ap().rearrange("p h v -> p (h v)")), 'tap', dbase('tap', 1), 1)
            dwait_total(s, 'tap')
        B.sync(tapsum)
    if upto == 'B1':
        fin_stop()
        return {}

    c_v = reserve('vec', 1)
    def c_v1(v):
        v.tensor_tensor(m8s[:, 0:1], MT[:, T - 1:T], BT[:, T - 1:T], ALU.subtract)
        v.tensor_copy(m8s[:, 1:2], BT[:, T - 1:T]).then_inc(sems['vec'], 1)
    B.vector(c_v1)

    def c_s1(s):
        s.wait_ge(sems['vec'], c_v + 1)
        dinc(s.dma_start(out=cc_in[:, 0:2056],
                         in_=CM.ap().rearrange("p h v -> p (h v)")), 'ccst', 0, 1)
        dinc(s.dma_start(out=cc_in[:, 2056:2064], in_=CM.ap()[:, 0, 0:8]), 'ccst', 1, 1)
        dwait(s, 'ccst', 1, 1)
        dinc(s.dma_start(out=cc_in[0:8, 2056:2058], in_=m8s[:, 0:2]), 'ccst', 2, 1)
    B.sync(c_s1)

    c_cc = reserve('cc', 1)
    def c_g1(gp):
        dwait_total(gp, 'ccst')
        gp.collective_compute(
            "AllGather", ALU.bypass,
            replica_groups=[list(range(NCORES))],
            ins=[cc_in], outs=[cc_out],
        ).then_inc(sems['cc'], 1)
    B.gpsimd(c_g1)

    if "t_sum" in tapd:
        def tapf(s):
            s.wait_ge(sems['cc'], c_cc + 1)
            dinc(s.dma_start(out=tapd["t_sum"], in_=cc_in[:, :]), 'tap', dbase('tap', 1), 1)
            dwait_total(s, 'tap')
        B.sync(tapf)

    def c_sl1(s):
        s.wait_ge(sems['cc'], c_cc + 1)
        src = cc_out.rearrange("(p r) c -> r p c", r=128)[0:8, 0:7, 2056:2058]
        dinc(s.dma_start(out=GS8.ap()[0:8, 400:414].rearrange("h (p q) -> h p q", q=2),
                         in_=src), 'mbld', 0, 1)
    B.sync(c_sl1)

    NV_C1 = 1 + 7 * 10 + 7
    c_v2b = reserve('vec', NV_C1)
    c_s2b = reserve('scl', 1)

    def c_v2(v):
        dwait(v, 'mbld', 0, 1)
        k = 0
        _W = True
        def step(fn):
            nonlocal k
            fn().then_inc(sems['vec'], 1); k += 1
            if _W:
                v.wait_ge(sems['vec'], c_v2b + k)
        v.memset(m8s[:, 2:3], -1.0e9)
        v.memset(m8s[:, 3:4], 0.0)
        v.sem_inc(sems['vec'], 1); k += 1
        if _W:
            v.wait_ge(sems['vec'], c_v2b + k)
        for p in range(7):
            u = umask[:, p:p + 1]
            Mp = GS8[0:8, 400 + 2 * p:401 + 2 * p]
            Bp = GS8[0:8, 401 + 2 * p:402 + 2 * p]
            step(lambda: v.tensor_scalar(m8s[:, 4:5], Bp, u, None, ALU.mult))
            step(lambda: v.tensor_scalar(m8s[:, 5:6], Mp, u, None, ALU.mult))
            step(lambda: v.tensor_scalar(m8s[:, 6:7], u, 2.0e9, -2.0e9, ALU.mult, ALU.add))
            step(lambda: v.tensor_tensor(m8s[:, 5:6], m8s[:, 5:6], m8s[:, 6:7], ALU.add))
            step(lambda: v.tensor_tensor(m8s[:, 6:7], m8s[:, 5:6], m8s[:, 3:4], ALU.subtract))
            step(lambda: v.tensor_tensor(m8s[:, 7:8], m8s[:, 2:3], m8s[:, 6:7], ALU.max))
            step(lambda: v.tensor_tensor(GS8[0:8, 416 + p:417 + p], m8s[:, 2:3], m8s[:, 7:8], ALU.subtract))
            step(lambda: v.tensor_tensor(GS8[0:8, 424 + p:425 + p], m8s[:, 6:7], m8s[:, 7:8], ALU.subtract))
            step(lambda: v.tensor_copy(m8s[:, 2:3], m8s[:, 7:8]))
            step(lambda: v.tensor_tensor(m8s[:, 3:4], m8s[:, 3:4], m8s[:, 4:5], ALU.add))
        step(lambda: v.tensor_scalar(GS8[0:8, 423:424], umask[:, 0:1], 0.0, None, ALU.mult))
        step(lambda: v.tensor_scalar_max(m8s[:, 6:7], m8s[:, 2:3], 0.0))
        step(lambda: v.tensor_tensor(MIN8[:, 0:1], m8s[:, 3:4], m8s[:, 6:7], ALU.add))
        step(lambda: v.tensor_tensor(GS8[0:8, 431:432], m8s[:, 2:3], m8s[:, 6:7], ALU.subtract))
        step(lambda: v.tensor_scalar_max(GS8[0:8, 416:432], GS8[0:8, 416:432], -80.0))
        step(lambda: v.tensor_copy(GS8[0:8, 448:455], umask[:, :]))
        step(lambda: v.tensor_copy(GS8[0:8, 456:464], m8s[:, :]))
        assert k == NV_C1, k
    B.vector(c_v2)

    def c_s2(s):
        s.wait_ge(sems['vec'], c_v2b + NV_C1)
        s.activation(GS8[:, 432:448], GS8[:, 416:432], ACT.Exp).then_inc(sems['scl'], 1)
        s.wait_ge(sems['scl'], c_s2b + 1)
        dinc(s.dma_start(out=rows_sp2, in_=GS8[0:8, 432:448]), 'r2st', 0, 1)
    B.scalar(c_s2)

    def c_g2(gp):
        dwait(gp, 'r2st', 0, 1)
        src = rows_sp2.rearrange("h c -> (h c)")
        dinc(gp.dma_start(out=BCA[:, 0:128], in_=src[None, :].partition_broadcast(128)),
             'bcbl', 0, 1)
    B.gpsimd(c_g2)

    def bcb(h, q):
        return BCA[:, h * 16 + q: h * 16 + q + 1]

    c_v3b = reserve('vec', 7 + 2)

    def c_sy3(s):
        s.wait_ge(sems['cc'], c_cc + 1)
        for p in range(7):
            if p >= 1:
                s.wait_ge(sems['vec'], c_v3b + 1 + p)
            dinc(s.dma_start(out=gbuf[:, :],
                             in_=cc_out[p * 128:(p + 1) * 128, :]), 'gbld', p, 1)
    B.sync(c_sy3)

    def c_v3(v):
        dwait(v, 'bcbl', 0, 1)
        v.memset(CM[:, :, :], 0.0)
        v.sem_inc(sems['vec'], 1)
        for p in range(7):
            dwait(v, 'gbld', p, 1)
            for h in range(NH):
                sl = slice(h * 257, (h + 1) * 257)
                v.tensor_scalar(gbuf[:, sl], gbuf[:, sl], bcb(h, 8 + p), None, ALU.mult)
            for h in range(NH):
                sl = slice(h * 257, (h + 1) * 257)
                op = v.scalar_tensor_tensor(CM[:, h, :], CM[:, h, :], bcb(h, p),
                                            gbuf[:, sl], ALU.mult, ALU.add)
            op.then_inc(sems['vec'], 1)
        for h in range(NH):
            op = v.tensor_scalar(CM[:, h, :], CM[:, h, :], bcb(h, 15), None, ALU.mult)
        op.then_inc(sems['vec'], 1)
    B.vector(c_v3)
    barrier()
    if upto == 'C' and "t_ent" in tapd:
        def tapent(s):
            s.wait_ge(sems['vec'], c_v3b + 9)
            b = dbase('tap', 1)
            dinc(s.dma_start(out=tapd["t_ent"], in_=CM.ap().rearrange("p h v -> p (h v)")), 'tap', b, 1)
            dinc(s.dma_start(out=tapd["t_min"], in_=MIN8[:, :]), 'tap', b + 1, 1)
            if "t_gs8" in tapd:
                dinc(s.dma_start(out=tapd["t_gs8"], in_=GS8[0:8, :]), 'tap', b + 2, 1)
            if "t_bca" in tapd:
                dinc(s.dma_start(out=tapd["t_bca"], in_=BCA[:, 0:128]), 'tap', b + 3, 1)
            dwait_total(s, 'tap')
        B.sync(tapent)
    if upto == 'C':
        fin_stop()
        return {}
    if "t_ent" in tapd:
        def tapf(s):
            b = dbase('tap', 1)
            dinc(s.dma_start(out=tapd["t_ent"], in_=CM.ap().rearrange("p h v -> p (h v)")), 'tap', b, 1)
            dinc(s.dma_start(out=tapd["t_min"], in_=MIN8[:, :]), 'tap', b + 1, 1)
            dwait_total(s, 'tap')
        B.sync(tapf)
        barrier()

    bca2 = emit_prep(True)
    vchb = dbase('vchl')

    def b2p_s(s):
        dinc(s.dma_start(out=qTs[:, :, :],
                         in_=q_sp.rearrange("h p t -> p h t")), 'qtl', 0, 1)
        dinc(s.dma_start(out=vch[:, 0, :], in_=v_sp[0:128, :]), 'vchl', vchb)
        dinc(s.dma_start(out=sogch[:, 0, :], in_=sog_sp[0:128, :]), 'sgl', 0)
    B.sync(b2p_s)

    def b2p_g(gp):
        for h in range(NH):
            dinc(gp.dma_start(out=gamma_bc[:, h * DV:(h + 1) * DV],
                              in_=gamma_i[h:h + 1, :].partition_broadcast(128)), 'gml', h, 1)
    B.gpsimd(b2p_g)
    barrier()

    NCHB = NCH
    M2, V2, S2, G2 = 56, 100, 28, 16
    def q_pm_S(h): return 12 * (h // 4) + 3 * (h % 4) + 1
    def q_pm_P1(h): return 12 * (h // 4) + 3 * (h % 4) + 2
    def q_pm_P2(h): return 12 * (h // 4) + 3 * (h % 4) + 3
    def q_pm_TRK(h): return 24 + 2 * h + 1
    def q_pm_SMM(h): return 24 + 2 * h + 2
    def q_pm_TRH(a): return 40 + a + 1
    def q_pv_C0A(h): return h + 1
    def q_pv_wst(h): return 8 + 18 * (h // 4) + 2 * (h % 4) + 1
    def q_pv_va(h): return 8 + 18 * (h // 4) + 2 * (h % 4) + 2
    def q_pv_gath(j, g2): return 8 + 18 * g2 + 8 + j + 1
    def q_pv_chain(k, g2): return 8 + 18 * g2 + 12 + k
    def q_pv_wk(h): return 44 + 2 * h + 1
    def q_pv_cm(h): return 44 + 2 * h + 2
    def q_pv_bns(h): return 60 + 3 * h + 1
    def q_pv_bna(h): return 60 + 3 * h + 2
    def q_pv_hm(h): return 60 + 3 * h + 3
    def q_pv_trhev(a): return 84 + a + 1
    def q_ps_chain(k, g2): return 6 * g2 + k
    def q_ps_hev(h): return 6 * (h // 4) + 2 + (h % 4) + 1
    def q_ps_rsln(h): return 12 + 2 * h + 1
    def q_ps_rsex(h): return 12 + 2 * h + 2
    def q_pg_gam(h): return 2 * h + 1
    def q_pg_sog(h): return 2 * h + 2

    b2m = reserve('mm', M2 * NCHB)
    b2v = reserve('vec', V2 * NCHB)
    b2s = reserve('scl', S2 * NCHB)
    b2g = reserve('gps', G2 * NCHB)

    def b2_sync(s):
        def hstore(c):
            hf, cl = divmod(c, 8)
            s.wait_ge(sems['vec'], b2v + V2 * c + q_pv_trhev(15))
            dinc(s.dma_start(out=hout_sp[hf, :, :, cl * 128:(cl + 1) * 128],
                             in_=houtTc[:, :, :]), 'hst', c, 1)
        for cn in range(1, NCHB):
            if cn >= 2:
                s.wait_ge(sems['vec'], b2v + V2 * (cn - 2) + q_pv_va(7))
            dinc(s.dma_start(out=vch[:, cn % 2, :],
                             in_=v_sp[cn * 128:(cn + 1) * 128, :]), 'vchl', vchb + cn)
            if cn >= 2:
                s.wait_ge(sems['gps'], b2g + G2 * (cn - 2) + q_pg_sog(7))
            dinc(s.dma_start(out=sogch[:, cn % 2, :],
                             in_=sog_sp[cn * 128:(cn + 1) * 128, :]), 'sgl', cn)
            hstore(cn - 1)
        hstore(NCHB - 1)
    B.sync(b2_sync)

    def b2_t(t):
        dwait(t, 'qtl', 0, 1)
        for c in range(NCHB):
            for g2 in range(2):
                for j in range(4):
                    h = 4 * g2 + j
                    if h == 0 and c >= 1:
                        t.wait_ge(sems['vec'], b2v + V2 * (c - 1) + V2)
                    if h >= 2:
                        t.wait_ge(sems['vec'], b2v + V2 * c + q_pv_wst(h - 2))
                    t.matmul(pbf(h % 2, 128), kT[:, h, c * 128:(c + 1) * 128],
                             qTs[:, h, c * 128:(c + 1) * 128],
                             start=True, stop=True).then_inc(sems['mm'], 1)
                    t.wait_ge(sems['vec'], b2v + V2 * c + q_pv_va(h))
                    if h >= 4:
                        t.wait_ge(sems['scl'], b2s + S2 * c + q_ps_hev(h - 4))
                        t.wait_ge(sems['vec'], b2v + V2 * c + q_pv_gath(h - 4, 0))
                    elif c >= 1:
                        t.wait_ge(sems['scl'], b2s + S2 * (c - 1) + q_ps_hev(h + 4))
                        t.wait_ge(sems['vec'], b2v + V2 * (c - 1) + q_pv_gath(h, 1))
                    t.matmul(pbf(4 + j, DV + 1), wstb[:, h % 2, :], vaug[:, h, :],
                             start=True, stop=False).then_inc(sems['mm'], 1)
                    t.wait_ge(sems['vec'], b2v + V2 * c + q_pv_C0A(h))
                    t.matmul(pbf(4 + j, DV + 1), qTs[:, h, c * 128:(c + 1) * 128],
                             C0A[:, h, :],
                             start=False, stop=True).then_inc(sems['mm'], 1)
            for h in range(NH):
                if h >= 1:
                    t.wait_ge(sems['vec'], b2v + V2 * c + q_pv_wk(h - 1))
                elif c >= 1:
                    t.wait_ge(sems['vec'], b2v + V2 * (c - 1) + q_pv_wk(7))
                t.transpose(pbb(2), kT[:, h, c * 128:(c + 1) * 128],
                            ident[:, :]).then_inc(sems['mm'], 1)
                t.wait_ge(sems['vec'], b2v + V2 * c + q_pv_wk(h))
                if h >= 1:
                    t.wait_ge(sems['vec'], b2v + V2 * c + q_pv_cm(h - 1))
                elif c >= 1:
                    t.wait_ge(sems['vec'], b2v + V2 * (c - 1) + q_pv_cm(7))
                t.matmul(pbf(3, DV + 1), wkb[:, h % 2, :], vaug[:, h, :],
                         start=True, stop=True).then_inc(sems['mm'], 1)
            for a in range(16):
                if a == 0:
                    t.wait_ge(sems['gps'], b2g + G2 * c + q_pg_sog(7))
                if a >= 2:
                    t.wait_ge(sems['vec'], b2v + V2 * c + q_pv_trhev(a - 2))
                t.transpose(pbb(a % 2), houtst[:, a * 128:(a + 1) * 128],
                            ident[:, :]).then_inc(sems['mm'], 1)
    B.tensor(b2_t)

    def b2_v(v):
        dwait(v, 'bcal', bca2, 1)
        dwait(v, 'gml', NH - 1, 1)
        for c in range(NCHB):
            kv = 0
            for h in range(NH):
                if c >= 1:
                    v.wait_ge(sems['mm'], b2m + M2 * (c - 1) + q_pm_P2(h))
                v.tensor_scalar(C0A[:, h, :], CM[:, h, :], bca_em0(h, c), None,
                                ALU.mult).then_inc(sems['vec'], 1); kv += 1
            for g2 in range(2):
                for j in range(4):
                    h = 4 * g2 + j
                    v.wait_ge(sems['mm'], b2m + M2 * c + q_pm_S(h))
                    if h >= 2:
                        v.wait_ge(sems['mm'], b2m + M2 * c + q_pm_P1(h - 2))
                    elif c >= 1:
                        v.wait_ge(sems['mm'], b2m + M2 * (c - 1) + q_pm_P1(h + 6))
                    v.scalar_tensor_tensor(wstb[:, h % 2, :], pbf(h % 2, 128),
                                           RC[:, c, h:h + 1], tri[:, :],
                                           ALU.mult, ALU.mult).then_inc(sems['vec'], 1); kv += 1
                    dwait(v, 'vchl', vchb + c)
                    if c >= 1:
                        v.wait_ge(sems['mm'], b2m + M2 * (c - 1) + q_pm_SMM(h))
                    v.tensor_copy(vaug[:, h, 0:DV],
                                  vch[:, c % 2, h * DV:(h + 1) * DV]).then_inc(sems['vec'], 1); kv += 1
                for j in range(4):
                    h = 4 * g2 + j
                    v.wait_ge(sems['mm'], b2m + M2 * c + q_pm_P2(h))
                    v.tensor_copy(rec4[:, 12 + j:13 + j],
                                  pbf(4 + j, DV + 1)[:, DV:DV + 1]).then_inc(sems['vec'], 1); kv += 1
                v.wait_ge(sems['vec'], b2v + V2 * c + q_pv_gath(3, g2))
                v.tensor_tensor(rec4[:, 8:12], rec4[:, 12:16], GALL[:, c, 4 * g2:4 * g2 + 4],
                                ALU.mult).then_inc(sems['vec'], 1); kv += 1
                v.wait_ge(sems['vec'], b2v + V2 * c + q_pv_chain(1, g2))
                v.tensor_scalar_mul(rec4[:, 12:16], rec4[:, 8:12], -1.0).then_inc(sems['vec'], 1); kv += 1
                v.wait_ge(sems['vec'], b2v + V2 * c + q_pv_chain(2, g2))
                v.tensor_tensor(rec4[:, 8:12], rec4[:, 8:12], rec4[:, 12:16], ALU.max).then_inc(sems['vec'], 1); kv += 1
                v.wait_ge(sems['vec'], b2v + V2 * c + q_pv_chain(3, g2))
                v.tensor_tensor(rec4[:, 8:12], rec4[:, 8:12], EMALL[:, c, 4 * g2:4 * g2 + 4],
                                ALU.max).then_inc(sems['vec'], 1); kv += 1
                v.wait_ge(sems['vec'], b2v + V2 * c + q_pv_chain(4, g2))
                v.tensor_scalar_add(rec4[:, 8:12], rec4[:, 8:12], MH_EPS).then_inc(sems['vec'], 1); kv += 1
                v.wait_ge(sems['vec'], b2v + V2 * c + q_pv_chain(5, g2))
                v.wait_ge(sems['scl'], b2s + S2 * c + q_ps_chain(2, g2))
                v.tensor_tensor(rec4[:, 4 * g2:4 * g2 + 4], rec4[:, 8:12],
                                GALL[:, c, 4 * g2:4 * g2 + 4], ALU.mult).then_inc(sems['vec'], 1); kv += 1
            for h in range(NH):
                v.wait_ge(sems['mm'], b2m + M2 * c + q_pm_TRK(h))
                if h >= 2:
                    v.wait_ge(sems['mm'], b2m + M2 * c + q_pm_SMM(h - 2))
                elif c >= 1:
                    v.wait_ge(sems['mm'], b2m + M2 * (c - 1) + q_pm_SMM(h + 6))
                v.tensor_scalar(wkb[:, h % 2, :], pbb(2), RC[:, c, h:h + 1],
                                bca_gl(h, c), ALU.mult, ALU.mult).then_inc(sems['vec'], 1); kv += 1
                v.wait_ge(sems['mm'], b2m + M2 * c + q_pm_SMM(h))
                v.scalar_tensor_tensor(CM[:, h, :], CM[:, h, :], bca_emg(h, c),
                                       pbf(3, DV + 1), ALU.mult, ALU.add).then_inc(sems['vec'], 1); kv += 1
            for h in range(NH):
                v.wait_ge(sems['scl'], b2s + S2 * c + q_ps_hev(h))
                v.bn_stats(stat6[:, 0:6], h_raw[:, h, :]).then_inc(sems['vec'], 1); kv += 1
                v.wait_ge(sems['vec'], b2v + V2 * c + q_pv_bns(h))
                if h >= 1:
                    v.wait_ge(sems['scl'], b2s + S2 * c + q_ps_rsln(h - 1))
                v.bn_aggr(stat6[:, 6:8], stat6[:, 0:6]).then_inc(sems['vec'], 1); kv += 1
                v.wait_ge(sems['vec'], b2v + V2 * c + q_pv_bna(h))
                v.wait_ge(sems['scl'], b2s + S2 * c + q_ps_rsex(h))
                if h >= 2:
                    v.wait_ge(sems['gps'], b2g + G2 * c + q_pg_gam(h - 2))
                elif c >= 1:
                    v.wait_ge(sems['gps'], b2g + G2 * (c - 1) + q_pg_gam(h + 6))
                v.tensor_scalar(hmb[:, h % 2, :], h_raw[:, h, :], stat6[:, 6:7],
                                srows[:, 2 + h % 2:3 + h % 2], ALU.subtract,
                                ALU.mult).then_inc(sems['vec'], 1); kv += 1
            for a in range(16):
                if a == 0 and c >= 1:
                    dwait(v, 'hst', c - 1, 1)
                v.wait_ge(sems['mm'], b2m + M2 * c + q_pm_TRH(a))
                v.tensor_copy(houtTc[:, a, :], pbb(a % 2)).then_inc(sems['vec'], 1); kv += 1
            assert kv == V2, kv
    B.vector(b2_v)

    def b2_s(s):
        for c in range(NCHB):
            ks = 0
            for g2 in range(2):
                s.wait_ge(sems['vec'], b2v + V2 * c + q_pv_chain(5, g2))
                s.activation(rec4[:, 12:16], rec4[:, 8:12], ACT.Ln).then_inc(sems['scl'], 1); ks += 1
                s.wait_ge(sems['scl'], b2s + S2 * c + q_ps_chain(1, g2))
                s.activation(rec4[:, 8:12], rec4[:, 12:16], ACT.Exp, scale=-1.0).then_inc(sems['scl'], 1); ks += 1
                for j in range(4):
                    h = 4 * g2 + j
                    s.wait_ge(sems['vec'], b2v + V2 * c + q_pv_chain(6, g2))
                    if c >= 1:
                        s.wait_ge(sems['vec'], b2v + V2 * (c - 1) + q_pv_hm(h))
                    s.activation(h_raw[:, h, :], pbf(4 + j, DV + 1)[:, 0:DV],
                                 ACT.Copy, scale=rec4[:, h:h + 1]).then_inc(sems['scl'], 1); ks += 1
            for h in range(NH):
                s.wait_ge(sems['vec'], b2v + V2 * c + q_pv_bna(h))
                if h >= 2:
                    s.wait_ge(sems['vec'], b2v + V2 * c + q_pv_hm(h - 2))
                s.activation(srows[:, 0:1], stat6[:, 7:8], ACT.Ln, bias=MH_EPS).then_inc(sems['scl'], 1); ks += 1
                s.wait_ge(sems['scl'], b2s + S2 * c + q_ps_rsln(h))
                s.activation(srows[:, 2 + h % 2:3 + h % 2], srows[:, 0:1],
                             ACT.Exp, scale=-0.5).then_inc(sems['scl'], 1); ks += 1
            assert ks == S2, ks
    B.scalar(b2_s)

    def b2_g(gp):
        for c in range(NCHB):
            for h in range(NH):
                gp.wait_ge(sems['vec'], b2v + V2 * c + q_pv_hm(h))
                if h == 0 and c >= 1:
                    gp.wait_ge(sems['mm'], b2m + M2 * c)
                gp.tensor_tensor(hgb[:, h % 2, :], hmb[:, h % 2, :],
                                 gamma_bc[:, h * DV:(h + 1) * DV], ALU.mult).then_inc(sems['gps'], 1)
                dwait(gp, 'sgl', c)
                gp.tensor_tensor(houtst[:, h * DV:(h + 1) * DV], hgb[:, h % 2, :],
                                 sogch[:, c % 2, h * DV:(h + 1) * DV], ALU.mult).then_inc(sems['gps'], 1)
    B.gpsimd(b2_g)
    barrier()
    if upto == 'B2':
        if "t_ho" in tapd:
            def tapho(s):
                b = dbase('tap', 1)
                dinc(s.dma_start(out=tapd["t_ho"],
                                 in_=hout_sp.rearrange("f p a n -> (f p) (a n)")),
                     'tap', b, 1)
                dwait_total(s, 'tap')
            B.sync(tapho)
        fin_stop()
        return {}

    for hf in range(2):
        def w_hload(s, hf=hf):
            dwait_total(s, 'hst')
            dinc(s.dma_start(out=houtW[:, :, :], in_=hout_sp[hf]), 'hwl', hf, 1)
        B.sync(w_hload)

        def w_plan(add, hf=hf):
            def dma_w(it):
                def fn(e, pos):
                    if it >= 2:
                        e.wait_ge(sems['mm'], pos('tensor', ('t', it - 1, 15)))
                    dinc(e.dma_start(out=wstW[:, it % 2, :],
                                     in_=Wout_i[it * 128:(it + 1) * 128, :]), 'wold', hf * 16 + it)
                    return None
                return fn

            def tr(it, a):
                def fn(t, pos):
                    if a == 0:
                        dwait(t, 'wold', hf * 16 + it)
                        if it >= 2:
                            t.wait_ge(sems['vec'], pos('vector', ('w', it - 2, 15)))
                    return t.transpose(pbt(it % 2, a), wstW[:, it % 2, a * 128:(a + 1) * 128],
                                       ident[:, :])
                return fn

            def wev(it, a):
                def fn(v, pos):
                    v.wait_ge(sems['mm'], pos('tensor', ('t', it, 7 if a < 8 else 15)))
                    vb, rr = divmod(it, 4)
                    return v.tensor_copy(WTw(vb % 2, a)[:, rr * 128:(rr + 1) * 128],
                                         pbt(it % 2, a))
                return fn

            def grp(ob, ttl):
                def fn(t, pos):
                    idx = ob * 8 + ttl
                    if idx == 0:
                        dwait(t, 'hwl', hf, 1)
                    if ttl == 0:
                        t.wait_ge(sems['vec'], pos('vector', ('w', ob * 4 + 3, 15)))
                    if idx >= 4:
                        t.wait_ge(sems['vec'], pos('vector', ('g', (idx - 4) // 8, (idx - 4) % 8)))
                    last = None
                    for a in range(16):
                        last = t.matmul(pbf(4 + idx % 4), houtW[:, a, ttl * 128:(ttl + 1) * 128],
                                        WTw(ob % 2, a),
                                        start=(a == 0), stop=(a == 15))
                    return last
                return fn

            def gev(ob, ttl):
                def fn(v, pos):
                    idx = ob * 8 + ttl
                    v.wait_ge(sems['mm'], pos('tensor', ('g', ob, ttl)))
                    dwait(v, 'xpst', hf * 32 + idx - 2)
                    return v.tensor_copy(ystW[:, idx % 2, :], pbf(4 + idx % 4))
                return fn

            for b in range(5):
                if b >= 1:
                    for ttl in range(8):
                        add('tensor', ('g', b - 1, ttl), grp(b - 1, ttl))
                if b < 4:
                    for rr in range(4):
                        it = b * 4 + rr
                        for a in range(16):
                            add('tensor', ('t', it, a), tr(it, a))
                if b >= 1:
                    for rr in range(4):
                        it = (b - 1) * 4 + rr
                        for a in range(16):
                            add('vector', ('w', it, a), wev(it, a))
                    for ttl in range(8):
                        add('vector', ('g', b - 1, ttl), gev(b - 1, ttl))
            for it in range(16):
                add('gpsimd', ('l', it), dma_w(it))
        w_pos = run_pipe(w_plan)

        def w_store(s, w_pos=w_pos, hf=hf):
            for ob in range(4):
                for ttl in range(8):
                    idx = ob * 8 + ttl
                    tt = hf * 8 + ttl
                    s.wait_ge(sems['vec'], w_pos('vector', ('g', ob, ttl)))
                    dinc(s.dma_start(out=xmid_sp[tt * 128:(tt + 1) * 128, ob * 512:(ob + 1) * 512],
                                     in_=ystW[:, idx % 2, :]), 'xpst', hf * 32 + idx)
        B.scalar(w_store)
        barrier()

        def l2_plan(add, hf=hf):
            def dma_ya(ttl):
                def fn(e, pos):
                    tt = hf * 8 + ttl
                    if ttl == 0:
                        dwait_total(e, 'xpst')
                    if ttl >= 2:
                        e.wait_ge(sems['vec'], pos('vector', ('n', ttl - 2)))
                    dinc(e.dma_start(out=xaW[:, ttl % 2, :],
                                     in_=xmid_sp[tt * 128:(tt + 1) * 128, :]), 'yal', hf * 8 + ttl)
                    return None
                return fn

            def dma_x(ttl):
                def fn(e, pos):
                    tt = hf * 8 + ttl
                    if ttl >= 2:
                        e.wait_ge(sems['vec'], pos('vector', ('a', ttl - 2)))
                    dinc(e.dma_start(out=xbW[:, ttl % 2, :],
                                     in_=x_in[tt * 128:(tt + 1) * 128, :]), 'xbl', hf * 8 + ttl)
                    return None
                return fn

            def addop(ttl):
                def fn(v, pos):
                    dwait(v, 'yal', hf * 8 + ttl)
                    dwait(v, 'xbl', hf * 8 + ttl)
                    dwait(v, 'xms', hf * 8 + ttl - 2)
                    return v.tensor_tensor(xaW[:, ttl % 2, :], xaW[:, ttl % 2, :],
                                           xbW[:, ttl % 2, :], ALU.add)
                return fn

            def statsp(ttl):
                def fn(v, pos):
                    v.wait_ge(sems['vec'], pos('vector', ('a', ttl)))
                    last = None
                    for q4 in range(4):
                        last = v.bn_stats(stat6[:, 6 * q4:6 * q4 + 6], xaW[:, ttl % 2, q4 * 512:(q4 + 1) * 512])
                    return last
                return fn

            def stats(ttl):
                def fn(v, pos):
                    v.wait_ge(sems['vec'], pos('vector', ('sp', ttl)))
                    return v.bn_aggr(stat6[:, 24:26], stat6[:, 0:24])
                return fn

            def rstd1(ttl):
                def fn(s, pos):
                    s.wait_ge(sems['vec'], pos('vector', ('s', ttl)))
                    return s.activation(srows[:, 1:2], stat6[:, 25:26], ACT.Ln, bias=LN_EPS)
                return fn

            def rstd(ttl):
                def fn(s, pos):
                    s.wait_ge(sems['scl'], pos('scalar', ('r1', ttl)))
                    return s.activation(srows[:, 0:1], srows[:, 1:2], ACT.Exp, scale=-0.5)
                return fn

            def xm_store(ttl):
                def fn(s, pos):
                    tt = hf * 8 + ttl
                    s.wait_ge(sems['vec'], pos('vector', ('a', ttl)))
                    dinc(s.dma_start(out=xmid_sp[tt * 128:(tt + 1) * 128, :],
                                     in_=xaW[:, ttl % 2, :]), 'xms', hf * 8 + ttl)
                    return None
                return fn

            def norm(ttl):
                def fn(v, pos):
                    v.wait_ge(sems['scl'], pos('scalar', ('r', ttl)))
                    if ttl >= 2:
                        v.wait_ge(sems['mm'], pos('tensor', ('t', ttl - 2, 15)))
                    return v.tensor_scalar(x2st[:, ttl % 2, :], xaW[:, ttl % 2, :],
                                           stat6[:, 24:25], srows[:, 0:1],
                                           ALU.subtract, ALU.mult)
                return fn

            def tr(ttl, a):
                def fn(t, pos):
                    if a == 0:
                        t.wait_ge(sems['vec'], pos('vector', ('n', ttl)))
                        if ttl >= 2:
                            t.wait_ge(sems['vec'], pos('vector', ('e', ttl - 2, 15)))
                    return t.transpose(pbt(ttl % 2, a), x2st[:, ttl % 2, a * 128:(a + 1) * 128],
                                       ident[:, :])
                return fn

            def ev(ttl, a):
                def fn(v, pos):
                    v.wait_ge(sems['mm'], pos('tensor', ('t', ttl, 7 if a < 8 else 15)))
                    return v.tensor_copy(x2T[:, a, ttl * 128:(ttl + 1) * 128], pbt(ttl % 2, a))
                return fn

            for ttl in range(8):
                add('sync', ('ya', ttl), dma_ya(ttl))
                add('sync', ('xb', ttl), dma_x(ttl))
                add('vector', ('a', ttl), addop(ttl))
                add('vector', ('sp', ttl), statsp(ttl))
                add('vector', ('s', ttl), stats(ttl))
                add('scalar', ('r1', ttl), rstd1(ttl))
                add('scalar', ('r', ttl), rstd(ttl))
                add('scalar', ('x', ttl), xm_store(ttl))
                add('vector', ('n', ttl), norm(ttl))
                if ttl >= 1:
                    for a in range(16):
                        add('vector', ('e', ttl - 1, a), ev(ttl - 1, a))
                for a in range(16):
                    add('tensor', ('t', ttl, a), tr(ttl, a))
            for a in range(16):
                add('vector', ('e', 7, a), ev(7, a))
        l2_pos = run_pipe(l2_plan)

        def l_out_f(s, l2_pos=l2_pos, hf=hf):
            s.wait_ge(sems['vec'], l2_pos('vector', ('e', 7, 15)))
            dinc(s.dma_start(out=xlt_sp[hf], in_=x2T[:, :, :]), 'xlst', hf, 1)
            dwait(s, 'xlst', hf, 1)
        B.sync(l_out_f)
        barrier()
        if upto == 'W' and hf == 1:
            fin_stop()
            return {}

    def f_pre(gp):
        dinc(gp.dma_start(out=gsus[:, 0:48],
                          in_=gate_si.rearrange("o (a p) -> p (o a)", p=128),
                          allow_slow_non_contiguous=True), 'fsc', 0, 1)
        dinc(gp.dma_start(out=gsus[:, 48:96],
                          in_=up_si.rearrange("o (a p) -> p (o a)", p=128),
                          allow_slow_non_contiguous=True), 'fsc', 1, 1)
    B.gpsimd(f_pre)

    for hf in range(2):
        def f_ld(s, hf=hf):
            dinc(s.dma_start(out=x2Tf[:, :, :], in_=xlt_sp[hf]), 'xtf', hf, 1)
        B.sync(f_ld)

        NGRP = 12 * 4 * 2
        f1_mm = reserve('mm', 12 * 16)
        f1_sc = reserve('scl', NGRP)
        f1_vu = reserve('vec', NGRP)
        f1_gp = reserve('gps', NGRP)

        def f1_g(g, f1_mm=f1_mm, f1_sc=f1_sc, f1_vu=f1_vu, hf=hf):
            def load(nb):
                for wi, W_ap in ((0, gate_w), (1, up_w)):
                    if nb >= 2:
                        g.wait_ge(sems['mm'], f1_mm + (nb - 2) * 16 + 15 + wi)
                    dst = (g8 if wi == 0 else u8)(nb % 2)
                    dinc(g.dma_start(out=dst,
                                     in_=W_ap.rearrange("(a p) n -> p a n", p=128)[:, :, nb * 512:(nb + 1) * 512]
                                     ), 'ggl' if wi == 0 else 'gul', hf * 12 + nb)
            def fuse_grp(nb):
                for nt4 in range(4):
                    for tb in range(2):
                        k = nb * 8 + nt4 * 2 + tb
                        nt = nb * 4 + nt4
                        g.wait_ge(sems['scl'], f1_sc + k + 1)
                        g.wait_ge(sems['vec'], f1_vu + k + 1)
                        g.tensor_tensor(fuse[:, nt, tb * 512:(tb + 1) * 512],
                                        gub[:, k % 2, :], uub[:, k % 2, :],
                                        ALU.mult).then_inc(sems['gps'], 1)
            load(0); load(1)
            for nb in range(2, 12):
                fuse_grp(nb - 2)
                load(nb)
            fuse_grp(10); fuse_grp(11)
        B.gpsimd(f1_g)

        def f2_gld(gp, hf=hf):
            dinc(gp.dma_start(out=ds_bc[:, :],
                              in_=down_si.partition_broadcast(128)), 'fsc', 2 + hf, 1)
        _F2GLD = f2_gld

        def f1_t(t, f1_sc=f1_sc, f1_vu=f1_vu, hf=hf):
            dwait(t, 'xtf', hf, 1)
            for nb in range(12):
                dwait(t, 'ggl', hf * 12 + nb)
                dwait(t, 'gul', hf * 12 + nb)
                for nt4 in range(4):
                    for tb in range(2):
                        k = nb * 8 + nt4 * 2 + tb
                        for wi in range(2):
                            srcf = (g8f if wi == 0 else u8f)
                            bank = (2 if wi == 0 else 4) + (k % 2)
                            if k >= 2:
                                if wi == 0:
                                    t.wait_ge(sems['scl'], f1_sc + k - 1)
                                else:
                                    t.wait_ge(sems['vec'], f1_vu + k - 1)
                            for a in range(16):
                                mmi = t.matmul(pbf(bank), srcf(nb % 2, a)[:, nt4 * 128:(nt4 + 1) * 128],
                                               x2Tf[:, a, tb * 512:(tb + 1) * 512],
                                               start=(a == 0), stop=(a == 15))
                                if a == 15:
                                    mmi.then_inc(sems['mm'], 1)
        B.tensor(f1_t)

        def f1_s(s, f1_mm=f1_mm, f1_gp=f1_gp):
            dwait(s, 'fsc', 0, 1)
            for nb in range(12):
                for nt4 in range(4):
                    for tb in range(2):
                        k = nb * 8 + nt4 * 2 + tb
                        s.wait_ge(sems['mm'], f1_mm + nb * 16 + (nt4 * 2 + tb) * 2 + 1)
                        if k >= 2:
                            s.wait_ge(sems['gps'], f1_gp + k - 1)
                        nt = nb * 4 + nt4
                        s.activation(gub[:, k % 2, :], pbf(2 + (k % 2)),
                                     ACT.Silu, scale=gsus[:, nt:nt + 1]).then_inc(sems['scl'], 1)
        B.scalar(f1_s)

        def f1_v(v, f1_mm=f1_mm, f1_gp=f1_gp):
            dwait(v, 'fsc', 1, 1)
            for nb in range(12):
                for nt4 in range(4):
                    for tb in range(2):
                        k = nb * 8 + nt4 * 2 + tb
                        v.wait_ge(sems['mm'], f1_mm + nb * 16 + (nt4 * 2 + tb) * 2 + 2)
                        if k >= 2:
                            v.wait_ge(sems['gps'], f1_gp + k - 1)
                        nt = nb * 4 + nt4
                        v.tensor_scalar(uub[:, k % 2, :], pbf(4 + (k % 2)),
                                        gsus[:, 48 + nt:49 + nt], None, ALU.mult).then_inc(sems['vec'], 1)
        B.vector(f1_v)

        barrier()
        B.gpsimd(_F2GLD)

        f2_mm = reserve('mm', 4 * NDT)
        f2_ev = reserve('vec', 4 * 8 * 2)

        def f2_sp(s, f2_mm=f2_mm, f2_ev=f2_ev, hf=hf):
            dwait_total(s, 'xms')
            dwait_total(s, 'dqst')
            def xaf_load(ttl):
                tt = hf * 8 + ttl
                if ttl >= 2:
                    prev = ttl - 2
                    gidx = (prev // 2) * 8 + (prev % 2) * 4 + 3
                    s.wait_ge(sems['vec'], f2_ev + (gidx + 1) * 2)
                dinc(s.dma_start(out=xaF[:, ttl % 2, :],
                                 in_=xmid_sp[tt * 128:(tt + 1) * 128, :]), 'xaf', hf * 8 + ttl)
            xaf_load(0); xaf_load(1)
            for tg in range(4):
                for nt in range(NDT):
                    li = hf * 4 * NDT + tg * NDT + nt
                    if tg * NDT + nt >= 2:
                        s.wait_ge(sems['mm'], f2_mm + tg * NDT + nt - 1)
                    dinc(s.dma_start(out=dn8[:, nt % 2, :],
                                     in_=down8_sp[nt]), 'dnl', li)
                for g2i in range(8):
                    gidx = tg * 8 + g2i
                    t2, ob = g2i // 4, g2i % 4
                    tt2 = hf * 8 + tg * 2 + t2
                    s.wait_ge(sems['vec'], f2_ev + tg * 16 + t2 * 8 + (ob // 2) * 4 + 3 + (ob % 2))
                    dinc(s.dma_start(out=y_out[tt2 * 128:(tt2 + 1) * 128, ob * 512:(ob + 1) * 512],
                                     in_=ystF[:, gidx % 2, :]), 'yst', hf * 32 + gidx)
                if tg < 3:
                    xaf_load(2 * tg + 2); xaf_load(2 * tg + 3)
        B.sync(f2_sp)

        def f2_t(t, f2_ev=f2_ev, hf=hf):
            for tg in range(4):
                if tg >= 1:
                    t.wait_ge(sems['vec'], f2_ev + tg * 16)
                for nt in range(NDT):
                    dwait(t, 'dnl', hf * 4 * NDT + tg * NDT + nt)
                    last = None
                    for t2 in range(2):
                        ttl = tg * 2 + t2
                        for ob in range(4):
                            bank = t2 * 4 + ob
                            last = t.matmul(pbf(bank), fuse[:, nt, ttl * 128:(ttl + 1) * 128],
                                            dn8[:, nt % 2, ob * 512:(ob + 1) * 512],
                                            start=(nt == 0), stop=(nt == NDT - 1))
                    last.then_inc(sems['mm'], 1)
        B.tensor(f2_t)

        def f2_v(v, f2_mm=f2_mm, hf=hf):
            dwait(v, 'fsc', 2 + hf, 1)
            for tg in range(4):
                v.wait_ge(sems['mm'], f2_mm + tg * NDT + NDT)
                for t2 in range(2):
                    ttl = tg * 2 + t2
                    dwait(v, 'xaf', hf * 8 + ttl)
                    for op in range(2):
                        gidx = tg * 8 + t2 * 4 + 2 * op
                        for gx in (gidx, gidx + 1):
                            ob = gx - tg * 8 - t2 * 4
                            if hf * 32 + gx - 2 >= 0:
                                dwait(v, 'yst', hf * 32 + gx - 2)
                            v.tensor_tensor(ystF[:, gx % 2, :], pbf(t2 * 4 + ob),
                                            ds_bc[:, ob * 512:(ob + 1) * 512],
                                            ALU.mult).then_inc(sems['vec'], 1)
                        for gx in (gidx, gidx + 1):
                            ob = gx - tg * 8 - t2 * 4
                            v.tensor_tensor(ystF[:, gx % 2, :], ystF[:, gx % 2, :],
                                            xaF[:, ttl % 2, ob * 512:(ob + 1) * 512],
                                            ALU.add).then_inc(sems['vec'], 1)
        B.vector(f2_v)

        barrier()

    def fin(s):
        for nm in ('yst', 'tap', 'xms', 'hst', 'xpst', 'xlst'):
            dwait_total(s, nm)
    B.sync(fin)
    return {}


_CACHED = None


def _get_nc():
    global _CACHED
    if _CACHED is None:
        _CACHED = build()
    return _CACHED


def kernel(**inputs):
    nc = _get_nc()
    in_maps = make_in_maps(inputs)
    res = run_bass_kernel_spmd(nc, in_maps, core_ids=list(range(8)))
    out = np.concatenate([np.asarray(res.results[c]["y"]) for c in range(8)], axis=0)
    return out.reshape(1, 16384, 2048).astype(np.float32, copy=False)


# version 03
# speedup vs baseline: 1.0579x; 1.0579x over previous
import contextlib
import numpy as np
import concourse.bass as bass
import concourse.mybir as mybir
from concourse.bass_utils import run_bass_kernel_spmd

F32 = mybir.dt.float32
BF16 = mybir.dt.bfloat16
FP8 = mybir.dt.float8e4
ALU = mybir.AluOpType
ACT = mybir.ActivationFunctionType

NCORES = 8
T = 2048
D = 2048
NH = 8
DV = 256
QK = 1024
NCH = 16
FFI = 6144
NTT = 16
LN_EPS = 1e-5
MH_EPS = 1e-6
QSCALE = 128.0 ** -0.5
NDT = FFI // 128


def build(taps=(), upto=None):
    nc = bass.Bass("TRN2", target_bir_lowering=False, detect_race_conditions=False)

    din = lambda name, shape, dtype=F32: nc.dram_tensor(name, list(shape), dtype, kind="ExternalInput").ap()
    dout = lambda name, shape, dtype=F32: nc.dram_tensor(name, list(shape), dtype, kind="ExternalOutput").ap()
    dint = lambda name, shape, dtype=F32: nc.dram_tensor(name, list(shape), dtype, kind="Internal").ap()

    x_in = din("x", (T, D))
    Wq = din("Wq", (QK, D)); Wk = din("Wk", (QK, D))
    Wv = din("Wv", (D, D)); Wog = din("Wog", (D, D)); Wout_i = din("Wout", (D, D))
    Wig = din("Wig", (NH, D)); Wfg = din("Wfg", (NH, D))
    big = din("big", (NH, 1)); bfg = din("bfg", (NH, 1))
    gamma_i = din("gamma", (NH, DV))
    gate_w = din("gate_w", (D, FFI)); up_w = din("up_w", (D, FFI)); down_w = din("down_w", (FFI, D))
    gate_si = din("gate_s", (1, FFI)); up_si = din("up_s", (1, FFI)); down_si = din("down_s", (1, D))
    ident_in = din("ident", (128, 128), BF16)
    id8_in = din("id8", (8, 8))
    tri_in = din("tri", (128, 128))
    umask_in = din("umask", (NH, NCORES - 1))

    y_out = dout("y", (T, D))

    q_sp = dint("q_sp", (NH, 128, T), BF16)
    v_sp = dint("v_sp", (T, D), BF16)
    sog_sp = dint("sog_sp", (T, D), BF16)
    hout_sp = dint("hout_sp", (2, 128, 16, T // 2), BF16)
    xmid_sp = dint("xmid_sp", (T, D))
    xlt_sp = dint("xlt_sp", (2, 128, 16, T // 2), BF16)
    down8_sp = dint("down8_sp", (NDT, 128, D), FP8)
    gw8_sp = dint("gw8_sp", (24, 128, 8192), FP8)
    rows_sp = dint("rows_sp", (NH, 48))
    rows_sp2 = dint("rows_sp2", (NH, 16))
    cc_in = dint("cc_in", (128, 2064))
    cc_out = dint("cc_out", (NCORES * 128, 2064))

    tapd = {}
    for tn, shape, dt in (
        ("t_xln", (128, 16, T), BF16), ("t_kT", (128, NH, T), BF16),
        ("t_i", (NH, T), F32), ("t_flog", (NH, T), F32), ("t_bt", (NH, T), F32),
        ("t_sum", (128, 2064), F32), ("t_ent", (128, NH * (DV + 1)), F32),
        ("t_min", (NH, 4), F32), ("t_mt", (NH, T), F32), ("t_gs8", (NH, 512), F32), ("t_bca", (128, 128), F32), ("t_ho", (256, 16384), BF16),
    ):
        if tn in taps:
            tapd[tn] = dout(tn, shape, dt)

    if True:
        sbg = lambda name, shape, dtype: nc.alloc_sbuf_tensor(name, list(shape), dtype)

        ident = sbg("s_ident", (128, 128), BF16)
        id8 = sbg("s_id8", (8, 8), F32)
        tri = sbg("s_tri", (128, 128), F32)
        onescol = sbg("onescol", (128, 1), BF16)
        ones1r = sbg("ones1r", (1, 128), F32)
        umask = sbg("s_umask", (NH, NCORES - 1), F32)
        bias15 = sbg("bias15", (NH, 2), F32)
        rowpack = sbg("rowpack", (NH, 48), F32)
        MIN8 = sbg("MIN8", (NH, 4), F32)
        rec4 = sbg("rec4", (128, 16), F32)
        stat6 = sbg("stat6", (128, 32), F32)
        srows = sbg("srows", (128, 4), F32)
        row16 = sbg("row16", (1, 16), F32)
        wkb = sbg("wkb", (128, 2, 128), BF16)
        wstb = sbg("wstb", (128, 2, 128), BF16)
        IH = sbg("IH", (NH, T), F32)
        FT = sbg("FT", (NH, T), F32)
        dq_stage = sbg("dq_stage", (128, 2, D), FP8)
        m8s = sbg("m8s", (NH, 8), F32)
        cbias = sbg("cbias", (128, 2), F32)
        nc.const_aps.register(F32, LN_EPS, cbias[:, 0:1])
        nc.const_aps.register(F32, MH_EPS, cbias[:, 1:2])

        rem = nc.sbuf_bytes_remaining
        AR = rem - 2048
        arena_h = nc.alloc_sbuf_tensor("arena", [128, AR], mybir.dt.uint8)
        ar_base = nc.lookup_mloc(arena_h).addr
        made = {}

        def at(name, shape, dtype, off):
            assert name not in made
            per = int(np.prod(shape[1:])) * mybir.dt.size(dtype)
            assert off + per <= AR, (name, off, per, AR)
            h = nc.alloc_sbuf_tensor_at(name, list(shape), dtype, offset=ar_base + off)
            made[name] = h
            return h

        def lay(*sizes):
            offs = []
            cur = None
            for s in sizes:
                if cur is None:
                    cur = s
                    continue
                offs.append(cur)
                cur += (s + 31) // 32 * 32
            offs.append(cur)
            return offs

        O_kT = 0
        AFTER_KT = 32768
        (O_xa, O_xlnst, O_xlnT, O_WTq, O_WTg, O_WTv, O_pst, A_END) = lay(
            AFTER_KT, 16384, 8192, 65536, 8192, 256, 32768, 2048)
        O_wst = O_xa
        (O_CM, O_C0A, O_RC, O_GALL, O_EMALL, O_BCA, O_GS8, O_FLOG, O_BT, O_MT,
         O_vch, O_sogch, O_qTs, O_gb, O_houtst, O_houtTc, O_gbuf, O_vaug,
         O_hraw, O_hmb, O_hgb, B_END) = lay(
            AFTER_KT, 8224, 4112, 512, 512, 512, 1536, 2048, 8192, 8192, 8192,
            8192, 8192, 32768, 8192, 4096, 4096, 8256, 4112, 2048, 2048, 2048)
        (O_wstW, O_houtW, O_WTw, O_xaW, O_ystW, O_x2st, O_x2T, O_xbW, W_END) = lay(
            0, 8192, 32768, 32768, 16384, 4096, 8192, 32768, 16384)
        (O_fuse, O_x2Tf, O_g8, O_u8, O_gub, O_uub, O_gsus, F1_END) = lay(
            0, 98304, 32768, 16384, 16384, 4096, 4096, 512)
        (O_dn8, O_dsb, O_xaF, O_ystF, F2_END) = lay(
            O_g8, 8192, 8192, 16384, 4096)
        assert max(A_END, B_END, W_END, F1_END, F2_END) <= AR, (A_END, B_END, W_END, F1_END, F2_END, AR)

        kT = at("kT", (128, NH, T), BF16, O_kT)
        xa = at("xa", (128, 2, D), F32, O_xa)
        wst = at("wst", (128, 2, D), BF16, O_wst)
        xlnst = at("xlnst", (128, 2, D), BF16, O_xlnst)
        xlnT = at("xlnT", (128, 16, T), BF16, O_xlnT)
        WTq4 = at("WTq", (128, 2 * 16 * 128), BF16, O_WTq)
        WTq = lambda b, a: WTq4[:, (b * 16 + a) * 128:(b * 16 + a + 1) * 128]
        WTg = at("WTg", (128, 16, 8), BF16, O_WTg)
        WTv4 = at("WTv", (128, 2 * 16 * 512), BF16, O_WTv)
        WTv = lambda b, a: WTv4[:, (b * 16 + a) * 512:(b * 16 + a + 1) * 512]
        pstage = at("pstage", (128, 2, 512), BF16, O_pst)

        CM = at("CM", (128, NH, DV + 1), F32, O_CM)
        C0A = at("C0A", (128, NH, DV + 1), BF16, O_C0A)
        RC = at("RC", (128, NCH, NH), F32, O_RC)
        GALL = at("GALL", (128, NCH, NH), F32, O_GALL)
        EMALL = at("EMALL", (128, NCH, NH), F32, O_EMALL)
        BCA = at("BCA", (128, 384), F32, O_BCA)
        GS8 = at("GS8", (8, 512), F32, O_GS8)
        FLOG = at("FLOG", (8, T), F32, O_FLOG)
        BT = at("BT", (8, T), F32, O_BT)
        MT = at("MT", (8, T), F32, O_MT)
        vch = at("vch", (128, 2, D), BF16, O_vch)
        sogch = at("sogch", (128, 2, D), BF16, O_sogch)
        qTs = at("qTs", (128, NH, T), BF16, O_qTs)
        gamma_bc = at("gamma_bc", (128, D), F32, O_gb)
        houtst = at("houtst", (128, D), BF16, O_houtst)
        houtTc = at("houtTc", (128, 16, 128), BF16, O_houtTc)
        gbuf = at("gbuf", (128, 2064), F32, O_gbuf)
        vaug = at("vaug", (128, NH, DV + 1), BF16, O_vaug)
        h_raw = at("h_raw", (128, NH, DV), F32, O_gbuf)
        hmb = at("hmb", (128, 2, DV), F32, O_hmb)
        hgb = at("hgb", (128, 2, DV), F32, O_hgb)

        wstW = at("wstW", (128, 2, D), BF16, O_wstW)
        houtW = at("houtW", (128, 16, T // 2), BF16, O_houtW)
        WTw4 = at("WTw", (128, 2 * 16 * 512), BF16, O_WTw)
        WTw = lambda b, a: WTw4[:, (b * 16 + a) * 512:(b * 16 + a + 1) * 512]
        xaW = at("xaW", (128, 2, D), F32, O_xaW)
        ystW = at("ystW", (128, 2, 512), F32, O_ystW)
        x2st = at("x2st", (128, 2, D), BF16, O_x2st)
        x2T = at("x2T", (128, 16, T // 2), BF16, O_x2T)
        xbW = at("xbW", (128, 2, D), F32, O_xbW)

        fuse = at("fuse", (128, NDT, T // 2), BF16, O_fuse)
        x2Tf = at("x2Tf", (128, 16, T // 2), BF16, O_x2Tf)
        g8r = at("g8", (128, 2 * 16 * 512), FP8, O_g8)
        g8 = lambda b: g8r.ap()[:, b * 8192:(b + 1) * 8192].rearrange("p (a n) -> p a n", a=16)
        g8f = lambda b, a: g8r[:, (b * 16 + a) * 512:(b * 16 + a + 1) * 512]
        u8r = at("u8", (128, 2 * 16 * 512), FP8, O_u8)
        u8 = lambda b: u8r.ap()[:, b * 8192:(b + 1) * 8192].rearrange("p (a n) -> p a n", a=16)
        u8f = lambda b, a: u8r[:, (b * 16 + a) * 512:(b * 16 + a + 1) * 512]
        gub = at("gub", (128, 2, 512), F32, O_gub)
        uub = at("uub", (128, 2, 512), F32, O_uub)
        gsus = at("gsus", (128, 128), F32, O_gsus)
        dn8 = at("dn8", (128, 4, D), FP8, O_dn8)
        ds_bc = at("ds_bc", (128, D), F32, O_dsb)
        xaF = at("xaF", (128, 2, D), F32, O_xaF)
        ystF = at("ystF", (128, 2, 512), F32, O_ystF)

        PB = [nc.alloc_psum_tensor(f"pb{k}", [128, 512], F32) for k in range(8)]

        def pbf(k, n=512):
            return PB[k][:, 0:n]

        def pbb(k, n=128):
            return PB[k].ap()[:, 0:(n + 1) // 2].bitcast(BF16)

        def pb8(k):
            return PB[k][0:8, 0:512]

        sems = {}
        ctr = {}
        for s in ('mm', 'vec', 'scl', 'gps', 'cc', 'bar'):
            sems[s] = nc.alloc_semaphore(name=s)
            ctr[s] = 0
        dstreams = {}

        def dget(name, depth=2):
            if name not in dstreams:
                ss = [nc.alloc_semaphore(name=f"{name}{k}") for k in range(depth)]
                dstreams[name] = dict(sems=ss, depth=depth, items=0)
            return dstreams[name]

        def dinc(instr, name, i, depth=2):
            st = dget(name, depth)
            instr.then_inc(st['sems'][i % st['depth']], 16)
            st['items'] = max(st['items'], i + 1)

        def dwait(eng, name, i, depth=2):
            if i < 0:
                return
            st = dget(name, depth)
            eng.wait_ge(st['sems'][i % st['depth']], 16 * (i // st['depth'] + 1))

        def dwait_total(eng, name):
            if name not in dstreams:
                return
            st = dstreams[name]
            n = st['items']
            d = st['depth']
            for k in range(d):
                cnt = (n - k + d - 1) // d
                if cnt > 0:
                    eng.wait_ge(st['sems'][k], 16 * cnt)

        def dbase(name, depth=2):
            return dget(name, depth)['items']
        blk_cm = nc.Block()
        B = blk_cm.__enter__()

        def reserve(s, n):
            b = ctr[s]
            ctr[s] += n
            return b

        def barrier():
            ctr['bar'] += 5
            tgt = ctr['bar']
            for secf in (B.sync, B.scalar, B.tensor, B.vector, B.gpsimd):
                def _f(e, tgt=tgt):
                    e.sem_inc(sems['bar'], 1)
                    e.wait_ge(sems['bar'], tgt)
                secf(_f)

        c0v = reserve('vec', 1)

        _scope_pa = nc.named_scope('PA'); _scope_pa.__enter__()
        def p0_g(g):
            dinc(g.dma_start(out=ident[:, :], in_=ident_in), 'p0c', 0, 1)
            dinc(g.dma_start(out=id8[:, :], in_=id8_in), 'p0c', 1, 1)
            dinc(g.dma_start(out=tri[:, :], in_=tri_in), 'p0c', 2, 1)
            dinc(g.dma_start(out=umask[:, :], in_=umask_in), 'p0c', 3, 1)
            dinc(g.dma_start(out=bias15[:, 0:1], in_=big), 'p0c', 4, 1)
            dinc(g.dma_start(out=bias15[:, 1:2], in_=bfg), 'p0c', 5, 1)
            g.memset(onescol[:, :], 1.0)
            g.memset(ones1r[:, :], 1.0)
            g.memset(srows[:, :], 0.0)
            g.memset(MIN8[:, :], 0.0)
            g.memset(cbias[:, 0:1], LN_EPS)
            g.memset(cbias[:, 1:2], MH_EPS)
        B.gpsimd(p0_g)

        def p0_v(v):
            dwait_total(v, 'p0c')
            v.tensor_scalar_mul(bias15[:, :], bias15[:, :], 1.0 / 15.0).then_inc(sems['vec'], 1)
        B.vector(p0_v)
        P0_B15 = c0v + 1


        DQ_DONE = None

        _stop = upto if isinstance(upto, str) else ''
        _order = ['A1', 'Aq', 'Aqk', 'Agates', 'Avog', 'A']
        _later = ['B1', 'C', 'B2', 'W']
        def _skip(stage):
            return _stop in _order and _order.index(_stop) < _order.index(stage)

        SE = {'tensor': 'mm', 'vector': 'vec', 'scalar': 'scl', 'gpsimd': 'gps'}

        def run_pipe(plan_fn):
            P = {'tensor': [], 'vector': [], 'scalar': [], 'gpsimd': [], 'sync': []}

            def add(eng, key, fn):
                P[eng].append((key, fn))
            plan_fn(add)
            bases = {eng: reserve(SE[eng], len(lst)) for eng, lst in P.items() if eng != 'sync'}
            posmap = {eng: {k: i + 1 for i, (k, _) in enumerate(lst)}
                      for eng, lst in P.items()}

            def pos(eng, key):
                return bases[eng] + posmap[eng][key]
            for eng, lst in P.items():
                if not lst:
                    continue
                def sec(e, lst=lst, eng=eng):
                    for k, fn in lst:
                        r = fn(e, pos)
                        if eng != 'sync':
                            if r is not None:
                                r.then_inc(sems[SE[eng]], 1)
                            else:
                                e.sem_inc(sems[SE[eng]], 1)
                getattr(B, eng)(sec)
            return pos

        def pbt(slot, a):
            bank = 2 * slot + a // 8
            off = (a % 8) * 64
            return PB[bank].ap()[:, off:off + 64].bitcast(BF16)

        def pbt8(a):
            off = a * 4
            return PB[0].ap()[:, off:off + 4].bitcast(BF16)

        def a1_plan(add):
            def dma_x(tt):
                def fn(e, pos):
                    if tt >= 2:
                        e.wait_ge(sems['vec'], pos('vector', ('n', tt - 2)))
                    dinc(e.dma_start(out=xa[:, tt % 2, :],
                                     in_=x_in[tt * 128:(tt + 1) * 128, :]), 'xld', tt)
                    return None
                return fn

            def statsp(tt):
                def fn(v, pos):
                    dwait(v, 'xld', tt)
                    last = None
                    for q4 in range(4):
                        last = v.bn_stats(stat6[:, 6 * q4:6 * q4 + 6], xa[:, tt % 2, q4 * 512:(q4 + 1) * 512])
                    return last
                return fn

            def stats(tt):
                def fn(v, pos):
                    v.wait_ge(sems['vec'], pos('vector', ('sp', tt)))
                    return v.bn_aggr(stat6[:, 24:26], stat6[:, 0:24])
                return fn

            def rstd(tt):
                def fn(s, pos):
                    s.wait_ge(sems['vec'], pos('vector', ('s', tt)))
                    s.activation(srows[:, 1:2], stat6[:, 25:26], ACT.Ln, bias=LN_EPS)
                    return s.activation(srows[:, 0:1], srows[:, 1:2], ACT.Exp, scale=-0.5)
                return fn

            def norm(tt):
                def fn(v, pos):
                    v.wait_ge(sems['scl'], pos('scalar', ('r', tt)))
                    if tt >= 2:
                        v.wait_ge(sems['mm'], pos('tensor', ('t', tt - 2, 15)))
                    return v.tensor_scalar(xlnst[:, tt % 2, :], xa[:, tt % 2, :],
                                           stat6[:, 24:25], srows[:, 0:1],
                                           ALU.subtract, ALU.mult)
                return fn

            def tr(tt, a):
                def fn(t, pos):
                    if tt == 0 and a == 0:
                        dwait_total(t, 'p0c')
                    if a == 0:
                        t.wait_ge(sems['vec'], pos('vector', ('n', tt)))
                        if tt >= 2:
                            t.wait_ge(sems['vec'], pos('vector', ('e', tt - 2, 15)))
                    return t.transpose(pbt(tt % 2, a), xlnst[:, tt % 2, a * 128:(a + 1) * 128],
                                       ident[:, :])
                return fn

            def ev(tt, a):
                def fn(v, pos):
                    v.wait_ge(sems['mm'], pos('tensor', ('t', tt, 7 if a < 8 else 15)))
                    return v.tensor_copy(xlnT[:, a, tt * 128:(tt + 1) * 128], pbt(tt % 2, a))
                return fn

            for tt in range(NTT):
                add('sync', ('x', tt), dma_x(tt))
                add('vector', ('sp', tt), statsp(tt))
                add('vector', ('s', tt), stats(tt))
                add('scalar', ('r', tt), rstd(tt))
                add('vector', ('n', tt), norm(tt))
                if tt >= 1:
                    for a in range(16):
                        add('vector', ('e', tt - 1, a), ev(tt - 1, a))
                for a in range(16):
                    add('tensor', ('t', tt, a), tr(tt, a))
            for a in range(16):
                add('vector', ('e', NTT - 1, a), ev(NTT - 1, a))
        run_pipe(a1_plan)

        barrier()
        if "t_xln" in tapd:
            def tapf(s):
                dinc(s.dma_start(out=tapd["t_xln"], in_=xlnT[:, :, :]), 'tap', dbase('tap', 1), 1)
                dwait_total(s, 'tap')
            B.sync(tapf)

        import os as _os2
        _nogrp = False
        _notr = False
        _nowev = False

        def emit_fm(W_ap, nblk, lname, gev_fn, gev_extra_wait=None):
            def plan(add):
                def dma_w(ofb):
                    def fn(e, pos):
                        if ofb >= 2:
                            e.wait_ge(sems['mm'], pos('tensor', ('t', ofb - 1, 15)))
                        dinc(e.dma_start(out=wst[:, ofb % 2, :],
                                         in_=W_ap[ofb * 128:(ofb + 1) * 128, :]), lname, ofb)
                        return None
                    return fn

                def tr(ofb, a):
                    def fn(t, pos):
                        if a == 0:
                            dwait(t, lname, ofb)
                            if ofb >= 2 and not _notr:
                                t.wait_ge(sems['vec'], pos('vector', ('w', ofb - 2, 15)))
                        if _notr:
                            return t.nop()
                        return t.transpose(pbt(ofb % 2, a), wst[:, ofb % 2, a * 128:(a + 1) * 128],
                                           ident[:, :])
                    return fn

                def wev(ofb, a):
                    def fn(v, pos):
                        v.wait_ge(sems['mm'], pos('tensor', ('t', ofb, 7 if a < 8 else 15)))
                        if _notr or _nowev:
                            return v.engine_nop()
                        return v.tensor_copy(WTq(ofb % 2, a), pbt(ofb % 2, a))
                    return fn

                def grp(ofb, tb):
                    def fn(t, pos):
                        if tb == 0:
                            t.wait_ge(sems['vec'], pos('vector', ('w', ofb, 15)))
                        if ofb >= 1:
                            t.wait_ge(sems['vec'], pos('vector', ('g', ofb - 1, tb)))
                        last = None
                        for a in range(16):
                            last = t.matmul(pbf(4 + tb), WTq(ofb % 2, a),
                                            xlnT[:, a, tb * 512:(tb + 1) * 512],
                                            start=(a == 0), stop=(a == 15))
                        return last
                    return fn

                def gev(ofb, tb):
                    def fn(v, pos):
                        v.wait_ge(sems['mm'], pos('tensor', ('g', ofb, tb)))
                        return gev_fn(v, pos, ofb, tb, pbf(4 + tb))
                    return fn

                for b in range(nblk + 1):
                    if b < nblk:
                        add('gpsimd', ('l', b), dma_w(b))
                    if b >= 1 and not _nogrp:
                        for tb in range(4):
                            add('tensor', ('g', b - 1, tb), grp(b - 1, tb))
                    if b < nblk:
                        for a in range(16):
                            add('tensor', ('t', b, a), tr(b, a))
                    if b >= 1:
                        for a in range(16):
                            add('vector', ('w', b - 1, a), wev(b - 1, a))
                        if not _nogrp:
                            for tb in range(4):
                                add('vector', ('g', b - 1, tb), gev(b - 1, tb))
            return run_pipe(plan)

        import os as _os
        _noqst = False
        if not _skip('Aq'):
            def q_gev(v, pos, ofb, tb, pp):
                idx = ofb * 4 + tb
                if not _noqst:
                    dwait(v, 'qst', idx - 2)
                return v.tensor_scalar_mul(pstage[:, idx % 2, :], pp, QSCALE)
            q_pos = emit_fm(Wq, QK // 128, 'wqld', q_gev)

            if not _noqst:
                def q_ssec(s):
                    for ofb in range(QK // 128):
                        for tb in range(4):
                            idx = ofb * 4 + tb
                            s.wait_ge(sems['vec'], q_pos('vector', ('g', ofb, tb)))
                            dinc(s.dma_start(out=q_sp[ofb, :, tb * 512:(tb + 1) * 512],
                                             in_=pstage[:, idx % 2, :]), 'qst', idx)
                B.scalar(q_ssec)
            barrier()

            pass
        if not _skip('Aqk'):
            def k_gev(v, pos, ofb, tb, pp):
                return v.tensor_copy(kT[:, ofb, tb * 512:(tb + 1) * 512], pp)
            emit_fm(Wk, QK // 128, 'wkld', k_gev)
            barrier()

        for gi, (Wg_ap, bcol, dstT) in enumerate((((Wig, 0, IH), (Wfg, 1, FT)) if not _skip('Agates') else ())):
            def g_plan(add, Wg_ap=Wg_ap, bcol=bcol, dstT=dstT, gi=gi):
                def dma_w(e, pos):
                    dinc(e.dma_start(out=wst[0:8, gi, :], in_=Wg_ap), 'gtld', gi)
                    return None
                add('gpsimd', 'l', dma_w)

                def tr(a):
                    def fn(t, pos):
                        if a == 0:
                            dwait(t, 'gtld', gi)
                        return t.transpose(pbt8(a), wst[0:8, gi, a * 128:(a + 1) * 128],
                                           ident[0:8, 0:8])
                    return fn

                def wev(a):
                    def fn(v, pos):
                        v.wait_ge(sems['mm'], pos('tensor', ('t', 15)))
                        return v.tensor_copy(WTg[:, a, :], pbt8(a))
                    return fn

                def grp(tb):
                    def fn(t, pos):
                        if tb == 0:
                            t.wait_ge(sems['vec'], pos('vector', ('w', 15)))
                        if gi >= 1 or tb >= 0:
                            pass
                        t.wait_ge(sems['scl'], pos('scalar', ('a', tb)) - 1)
                        last = None
                        for a in range(16):
                            last = t.matmul(pb8(4 + tb), WTg[:, a, :],
                                            xlnT[:, a, tb * 512:(tb + 1) * 512],
                                            start=(a == 0), stop=(a == 15))
                        return last
                    return fn

                def act(tb):
                    def fn(s, pos):
                        if tb == 0:
                            s.wait_ge(sems['vec'], P0_B15)
                        s.wait_ge(sems['mm'], pos('tensor', ('g', tb)))
                        return s.activation(dstT[:, tb * 512:(tb + 1) * 512], pb8(4 + tb), ACT.Tanh,
                                            bias=bias15[:, bcol:bcol + 1], scale=1.0 / 15.0)
                    return fn

                for a in range(16):
                    add('tensor', ('t', a), tr(a))
                for a in range(16):
                    add('vector', ('w', a), wev(a))
                for tb in range(4):
                    add('tensor', ('g', tb), grp(tb))
                    add('scalar', ('a', tb), act(tb))
            run_pipe(g_plan)
            barrier()

        for W_ap, kind in (((Wv, 'v'), (Wog, 'og')) if not _skip('Avog') else ()):
            lname = 'wvld' if kind == 'v' else 'wogld'
            sname = 'vst' if kind == 'v' else 'ogst'

            def a4_plan(add, W_ap=W_ap, kind=kind, lname=lname, sname=sname):
                def dma_w(it):
                    def fn(e, pos):
                        if it >= 2:
                            e.wait_ge(sems['mm'], pos('tensor', ('t', it - 1, 15)))
                        dinc(e.dma_start(
                            out=wst[:, it % 2, :],
                            in_=W_ap[it * 128:(it + 1) * 128, :]), lname, it)
                        return None
                    return fn

                def tr(it, a):
                    def fn(t, pos):
                        if a == 0:
                            dwait(t, lname, it)
                            if it >= 2:
                                t.wait_ge(sems['vec'], pos('vector', ('w', it - 2, 15)))
                        return t.transpose(pbt(it % 2, a), wst[:, it % 2, a * 128:(a + 1) * 128],
                                           ident[:, :])
                    return fn

                def wev(it, a):
                    def fn(v, pos):
                        v.wait_ge(sems['mm'], pos('tensor', ('t', it, 7 if a < 8 else 15)))
                        vb, rr = divmod(it, 4)
                        return v.tensor_copy(WTv(vb % 2, a)[:, rr * 128:(rr + 1) * 128],
                                             pbt(it % 2, a))
                    return fn

                def grp(vb, tt):
                    def fn(t, pos):
                        if tt == 0:
                            t.wait_ge(sems['vec'], pos('vector', ('w', vb * 4 + 3, 15)))
                        idx = vb * 16 + tt
                        if idx >= 4:
                            if kind == 'v':
                                t.wait_ge(sems['vec'], pos('vector', ('g', (idx - 4) // 16, (idx - 4) % 16)))
                            else:
                                t.wait_ge(sems['scl'], pos('scalar', ('g', (idx - 4) // 16, (idx - 4) % 16)))
                        last = None
                        for a in range(16):
                            last = t.matmul(pbf(4 + idx % 4), xlnT[:, a, tt * 128:(tt + 1) * 128],
                                            WTv(vb % 2, a),
                                            start=(a == 0), stop=(a == 15))
                        return last
                    return fn

                def gev(vb, tt):
                    idx = vb * 16 + tt
                    if kind == 'v':
                        def fn(v, pos):
                            v.wait_ge(sems['mm'], pos('tensor', ('g', vb, tt)))
                            dwait(v, sname, idx - 2)
                            return v.tensor_copy(pstage[:, idx % 2, :], pbf(4 + idx % 4))
                        return ('vector', fn)
                    else:
                        def fn(s, pos):
                            s.wait_ge(sems['mm'], pos('tensor', ('g', vb, tt)))
                            dwait(s, sname, idx - 2)
                            return s.activation(pstage[:, idx % 2, :], pbf(4 + idx % 4), ACT.Sigmoid)
                        return ('scalar', fn)

                for vb in range(5):
                    if vb >= 1:
                        for tt in range(NTT):
                            add('tensor', ('g', vb - 1, tt), grp(vb - 1, tt))
                    if vb < 4:
                        for rr in range(4):
                            it = vb * 4 + rr
                            for a in range(16):
                                add('tensor', ('t', it, a), tr(it, a))
                    if vb >= 1:
                        for rr in range(4):
                            it = (vb - 1) * 4 + rr
                            for a in range(16):
                                add('vector', ('w', it, a), wev(it, a))
                    if vb >= 1:
                        for tt in range(NTT):
                            eng, fn = gev(vb - 1, tt)
                            add(eng, ('g', vb - 1, tt), fn)
                for it in range(16):
                    add('gpsimd', ('l', it), dma_w(it))
            a4_pos = run_pipe(a4_plan)

            if kind == 'v':
                def v_store(s, a4_pos=a4_pos):
                    for vb in range(4):
                        for tt in range(NTT):
                            idx = vb * 16 + tt
                            s.wait_ge(sems['vec'], a4_pos('vector', ('g', vb, tt)))
                            dinc(s.dma_start(
                                out=v_sp[tt * 128:(tt + 1) * 128, vb * 512:(vb + 1) * 512],
                                in_=pstage[:, idx % 2, :]), 'vst', idx)
                B.scalar(v_store)
                barrier()
            else:
                def og_store(s, a4_pos=a4_pos):
                    for vb in range(4):
                        for tt in range(NTT):
                            idx = vb * 16 + tt
                            s.wait_ge(sems['scl'], a4_pos('scalar', ('g', vb, tt)))
                            dinc(s.dma_start(
                                out=sog_sp[tt * 128:(tt + 1) * 128, vb * 512:(vb + 1) * 512],
                                in_=pstage[:, idx % 2, :]), 'ogst', idx)
                B.sync(og_store)
            barrier()
        if "t_i" in tapd:
            def tapf(s):
                dinc(s.dma_start(out=tapd["t_i"], in_=IH[:, :]), 'tap', dbase('tap', 1), 1)
                dwait_total(s, 'tap')
            B.sync(tapf)
        if "t_kT" in tapd:
            def tapf(s):
                dinc(s.dma_start(out=tapd["t_kT"], in_=kT[:, :, :]), 'tap', dbase('tap', 1), 1)
                dwait_total(s, 'tap')
            B.sync(tapf)

        if _stop in _order:
            def fin_a(s):
                for nm in list(dstreams):
                    dwait_total(s, nm)
            B.sync(fin_a)
        else:
            _scope_pa.__exit__(None, None, None)
        emit_rest(locals())
        blk_cm.__exit__(None, None, None)
    return nc


import ml_dtypes

def make_in_maps(full):
    x = np.asarray(full["L0_attn_x"], np.float32).reshape(NCORES * T, D)
    ident = np.eye(128, dtype=ml_dtypes.bfloat16)
    id8 = np.eye(8, dtype=np.float32)
    tri = (np.arange(128)[:, None] <= np.arange(128)[None, :]).astype(np.float32)
    base = dict(
        Wq=np.asarray(full["L0_attn_Wq"], np.float32),
        Wk=np.asarray(full["L0_attn_Wk"], np.float32),
        Wv=np.asarray(full["L0_attn_Wv"], np.float32),
        Wog=np.asarray(full["L0_attn_Wog"], np.float32),
        Wout=np.asarray(full["L0_attn_Wout"], np.float32),
        Wig=np.asarray(full["L0_attn_Wig"], np.float32),
        Wfg=np.asarray(full["L0_attn_Wfg"], np.float32),
        big=np.asarray(full["L0_attn_big"], np.float32).reshape(NH, 1),
        bfg=np.asarray(full["L0_attn_bfg"], np.float32).reshape(NH, 1),
        gamma=np.asarray(full["L0_attn_gamma"], np.float32),
        gate_w=np.asarray(full["L0_ffn_gate_w"], np.float32),
        up_w=np.asarray(full["L0_ffn_up_w"], np.float32),
        down_w=np.asarray(full["L0_ffn_down_w"], np.float32),
        gate_s=np.asarray(full["L0_ffn_gate_scale"], np.float32).reshape(1, FFI),
        up_s=np.asarray(full["L0_ffn_up_scale"], np.float32).reshape(1, FFI),
        down_s=np.asarray(full["L0_ffn_down_scale"], np.float32).reshape(1, D),
        ident=ident, id8=id8, tri=tri,
    )
    in_maps = []
    for core in range(NCORES):
        m = dict(base)
        m["x"] = np.ascontiguousarray(x[core * T:(core + 1) * T])
        m["umask"] = np.broadcast_to(
            (np.arange(NCORES - 1)[None, :] < core).astype(np.float32), (NH, NCORES - 1)).copy()
        in_maps.append(m)
    return in_maps


"""Phases B1 (pass-1), C (exchange), B2 (pass-2), W (Wout), L2 (xmid LN), F (FFN)."""

F32 = mybir.dt.float32
BF16 = mybir.dt.bfloat16
FP8 = mybir.dt.float8e4
ALU = mybir.AluOpType
ACT = mybir.ActivationFunctionType

NCORES = 8
T = 2048
D = 2048
NH = 8
DV = 256
NCH = 16
FFI = 6144
NTT = 16
LN_EPS = 1e-5
MH_EPS = 1e-6
NDT = 48


def emit_rest(L):
    gget = lambda n: L[n]
    upto = L.get('upto')

    def fin_stop(dstreams=None):
        dwait_total = L['dwait_total']
        B = L['B']
        def fin(s):
            for nm in list(L['dstreams']):
                dwait_total(s, nm)
        B.sync(fin)
    nc = gget('nc'); B = gget('B'); sems = gget('sems'); ctr = gget('ctr')
    reserve = gget('reserve'); barrier = gget('barrier')
    pbf = gget('pbf'); pbb = gget('pbb'); PB = gget('PB')
    run_pipe = gget('run_pipe'); pbt = gget('pbt')
    dinc = gget('dinc'); dwait = gget('dwait'); dwait_total = gget('dwait_total'); dbase = gget('dbase')
    tapd = gget('tapd'); P0_B15 = gget('P0_B15'); DQ_DONE = gget('DQ_DONE')
    for _n in ('ident', 'id8', 'tri', 'onescol', 'ones1r', 'umask', 'bias15',
               'rowpack', 'MIN8', 'rec4', 'stat6', 'srows', 'row16', 'wkb', 'wstb',
               'IH', 'FT', 'dq_stage', 'm8s', 'kT', 'CM', 'C0A', 'RC', 'GALL',
               'EMALL', 'BCA', 'GS8', 'FLOG', 'BT', 'MT', 'vch', 'sogch', 'qTs',
               'gamma_bc', 'houtst', 'houtTc', 'gbuf', 'vaug', 'h_raw', 'hmb', 'hgb',
               'wstW', 'houtW', 'WTw', 'xaW', 'xbW', 'ystW', 'x2st', 'x2T',
               'fuse', 'x2Tf', 'g8', 'u8', 'g8f', 'u8f', 'g8r', 'u8r', 'gw8_sp', 'gub', 'uub', 'gsus', 'dn8', 'ds_bc',
               'xaF', 'ystF',
               'x_in', 'Wout_i', 'gamma_i', 'gate_w', 'up_w', 'down_w',
               'gate_si', 'up_si', 'down_si',
               'y_out', 'q_sp', 'v_sp', 'sog_sp', 'hout_sp', 'xmid_sp', 'xlt_sp',
               'down8_sp', 'rows_sp', 'rows_sp2', 'cc_in', 'cc_out'):
        globals()[_n] = gget(_n)

    MTtail = MT.ap().rearrange("h (c l) -> h c l", l=128)

    binit = reserve('gps', 1)

    def b_init(gp):
        gp.memset(CM[:, :, :], 0.0)
        for h in range(NH):
            gp.memset(vaug[:, h, DV:DV + 1], 1.0)
        gp.sem_inc(sems['gps'], 1)
    B.gpsimd(b_init)
    B_INIT = binit + 1

    def emit_prep(is_pass2):
        NVpro = 1 if is_pass2 else 5
        NSpro = 0 if is_pass2 else 2
        NV = NVpro + NCH * 7 + 2
        NS = NSpro + NCH * 3 + 1
        NM = NCH * 3
        ipass = 1 if is_pass2 else 0
        pv = reserve('vec', NV)
        psc = reserve('scl', NS)
        pm = reserve('mm', NM)

        def vsec(v):
            if not is_pass2:
                v.wait_ge(sems['scl'], psc + 1)
                v.memset(FT[:, :], 0.0)
                v.tensor_scalar_add(FLOG[:, :], FLOG[:, :], 1.0).then_inc(sems['vec'], 1)
                v.tensor_scalar_mul(IH[:, :], IH[:, :], 15.0).then_inc(sems['vec'], 1)
                v.wait_ge(sems['scl'], psc + 2)
                v.tensor_scalar_mul(FLOG[:, :], MT[:, :], -1.0).then_inc(sems['vec'], 1)
                v.wait_ge(sems['vec'], pv + 3)
                v.tensor_tensor_scan(BT[:, :], FLOG[:, :], FT[:, :], 0.0,
                                     ALU.add, ALU.add).then_inc(sems['vec'], 1)
                v.tensor_tensor_scan(MT[:, :], FLOG[:, :], IH[:, :], -60.0,
                                     ALU.add, ALU.max).then_inc(sems['vec'], 1)
            else:
                v.tensor_tensor_scan(MT[:, :], FLOG[:, :], IH[:, :], MIN8[:, 0:1],
                                     ALU.add, ALU.max).then_inc(sems['vec'], 1)
            for c in range(NCH):
                sl = slice(c * 128, (c + 1) * 128)
                beta = MIN8[:, 3:4] if c == 0 else BT[:, c * 128 - 1:c * 128]
                v.tensor_scalar(GS8[:, 0:128], BT[:, sl], beta, None,
                                ALU.subtract).then_inc(sems['vec'], 1)
                v.wait_ge(sems['vec'], pv + NVpro + c * 7 + 1)
                v.tensor_tensor(GS8[:, 128:256], IH[:, sl], GS8[:, 0:128],
                                ALU.subtract).then_inc(sems['vec'], 1)
                v.tensor_tensor(GS8[:, 256:384], GS8[:, 0:128], MT[:, sl],
                                ALU.subtract).then_inc(sems['vec'], 1)
                v.wait_ge(sems['mm'], pm + c * 3 + 1)
                v.tensor_copy(RC[:, c, :], pbf(4, 8)).then_inc(sems['vec'], 1)
                v.wait_ge(sems['mm'], pm + c * 3 + 2)
                v.tensor_copy(GALL[:, c, :], pbf(4, 8)).then_inc(sems['vec'], 1)
                v.wait_ge(sems['mm'], pm + c * 3 + 3)
                v.tensor_copy(EMALL[:, c, :], pbf(4, 8)).then_inc(sems['vec'], 1)
                v.tensor_copy(rowpack[:, 16 + c:17 + c], GS8[0:8, 255:256]).then_inc(sems['vec'], 1)
            if is_pass2:
                v.tensor_copy(GS8[:, 384:385], MIN8[:, 0:1])
            else:
                v.memset(GS8[:, 384:385], -60.0)
            v.tensor_copy(GS8[:, 385:400], MTtail[:, 0:15, 127]).then_inc(sems['vec'], 1)
            v.wait_ge(sems['scl'], psc + NS)
            v.wait_ge(sems['vec'], pv + NV - 1)
            v.tensor_tensor(rowpack[:, 32:48], rowpack[:, 0:16], rowpack[:, 16:32],
                            ALU.mult).then_inc(sems['vec'], 1)
        B.vector(vsec)

        def ssec(s):
            if not is_pass2:
                s.wait_ge(sems['vec'], P0_B15)
                s.activation(FLOG[:, :], FT[:, :], ACT.Exp, scale=-15.0).then_inc(sems['scl'], 1)
                s.wait_ge(sems['vec'], pv + 1)
                s.activation(MT[:, :], FLOG[:, :], ACT.Ln).then_inc(sems['scl'], 1)
            for c in range(NCH):
                sl = slice(c * 128, (c + 1) * 128)
                s.wait_ge(sems['vec'], pv + NVpro + c * 7 + 3)
                s.activation(GS8[:, 0:128], GS8[:, 128:256], ACT.Exp).then_inc(sems['scl'], 1)
                s.wait_ge(sems['mm'], pm + c * 3 + 1)
                s.activation(GS8[:, 128:256], GS8[:, 256:384], ACT.Exp).then_inc(sems['scl'], 1)
                s.wait_ge(sems['mm'], pm + c * 3 + 2)
                s.activation(GS8[:, 256:384], MT[:, sl], ACT.Exp, scale=-1.0).then_inc(sems['scl'], 1)
            s.wait_ge(sems['vec'], pv + NV - 1)
            s.activation(rowpack[:, 0:16], GS8[:, 384:400], ACT.Exp).then_inc(sems['scl'], 1)
            s.wait_ge(sems['vec'], pv + NV)
            dinc(s.dma_start(out=rows_sp, in_=rowpack[:, :]), 'rowst', ipass, 1)
        B.scalar(ssec)

        def tsec(t):
            for c in range(NCH):
                t.wait_ge(sems['scl'], psc + NSpro + c * 3 + 1)
                if c > 0:
                    t.wait_ge(sems['vec'], pv + NVpro + (c - 1) * 7 + 6)
                t.matmul(pbf(4, 8), GS8[0:8, 0:128], id8[:, :],
                         start=True, stop=True).then_inc(sems['mm'], 1)
                t.wait_ge(sems['scl'], psc + NSpro + c * 3 + 2)
                t.wait_ge(sems['vec'], pv + NVpro + c * 7 + 4)
                t.matmul(pbf(4, 8), GS8[0:8, 128:256], id8[:, :],
                         start=True, stop=True).then_inc(sems['mm'], 1)
                t.wait_ge(sems['scl'], psc + NSpro + c * 3 + 3)
                t.wait_ge(sems['vec'], pv + NVpro + c * 7 + 5)
                t.matmul(pbf(4, 8), GS8[0:8, 256:384], id8[:, :],
                         start=True, stop=True).then_inc(sems['mm'], 1)
        B.tensor(tsec)

        def gsec(gp):
            dwait(gp, 'rowst', ipass, 1)
            dinc(gp.dma_start(out=BCA[:, :],
                              in_=rows_sp.rearrange("h c -> (h c)")[None, :].partition_broadcast(128)
                              ), 'bcal', ipass, 1)
        B.gpsimd(gsec)
        return ipass

    def bca_em0(h, c): return BCA[:, h * 48 + c: h * 48 + c + 1]
    def bca_gl(h, c): return BCA[:, h * 48 + 16 + c: h * 48 + 17 + c]
    def bca_emg(h, c): return BCA[:, h * 48 + 32 + c: h * 48 + 33 + c]

    def cast_one(g, j):
        if j < 4:
            dinc(g.dma_start(
                out=down8_sp[12 * j:12 * (j + 1)].rearrange("t p n -> t p n"),
                in_=down_w[1536 * j:1536 * (j + 1)].rearrange("(t p) n -> t p n", p=128)),
                'dqst', j, 1)
        elif j < 28:
            wi, nb = divmod(j - 4, 12)
            W_ap = gate_w if wi == 0 else up_w
            W_r = W_ap.rearrange("(a p) n -> p a n", p=128)
            dinc(g.dma_start(out=gw8_sp[j - 4].rearrange("p (a n) -> p a n", a=16),
                             in_=W_r[:, :, nb * 512:(nb + 1) * 512]), 'dqst', j, 1)

    _sc = nc.named_scope('B1'); _sc.__enter__()


    bca_p1 = emit_prep(False)

    def b1_casts(g):
        for j in range(6):
            cast_one(g, j)
    B.gpsimd(b1_casts)
    barrier()
    if "t_flog" in tapd:
        def tapf(s):
            b = dbase('tap', 1)
            dinc(s.dma_start(out=tapd["t_flog"], in_=FLOG[:, :]), 'tap', b, 1)
            dinc(s.dma_start(out=tapd["t_bt"], in_=BT[:, :]), 'tap', b + 1, 1)
            dinc(s.dma_start(out=tapd["t_mt"], in_=MT[:, :]), 'tap', b + 2, 1)
            dwait_total(s, 'tap')
        B.sync(tapf)
        barrier()

    def p1m_trk(i): return 16 * (i // 8) + 2 * (i % 8) + 1
    def p1m_smm(i): return 16 * (i // 8) + 2 * (i % 8) + 2
    def p1v_wk(i):
        c, h = divmod(i, 8)
        return 24 * c + (1 if h == 0 else 3 * h)
    def p1v_va(i):
        c, h = divmod(i, 8)
        return 24 * c + (2 if h == 0 else 3 * h + 1)
    def p1v_cm(i):
        c, h = divmod(i, 8)
        return 24 * c + (24 if h == 7 else 3 * h + 5)

    b1m = reserve('mm', 16 * NCH)
    b1v = reserve('vec', 24 * NCH)

    def b1_sync(s):
        for c in range(NCH):
            if c >= 2:
                s.wait_ge(sems['vec'], b1v + p1v_va(8 * (c - 2) + 7))
            dinc(s.dma_start(out=vch[:, c % 2, :],
                             in_=v_sp[c * 128:(c + 1) * 128, :]), 'vchl', c)
    B.sync(b1_sync)

    def b1_t(t):
        t.wait_ge(sems['gps'], B_INIT)
        for i in range(8 * NCH):
            c, h = divmod(i, 8)
            if i >= 2:
                t.wait_ge(sems['vec'], b1v + p1v_wk(i - 2))
            t.transpose(pbb(i % 2), kT[:, h, c * 128:(c + 1) * 128],
                        ident[:, :]).then_inc(sems['mm'], 1)
            t.wait_ge(sems['vec'], b1v + p1v_va(i))
            if i >= 2:
                t.wait_ge(sems['vec'], b1v + p1v_cm(i - 2))
            t.matmul(pbf(2 + i % 2, DV + 1), wkb[:, i % 2, :], vaug[:, h, :],
                     start=True, stop=True).then_inc(sems['mm'], 1)
    B.tensor(b1_t)

    def b1_v(v):
        dwait(v, 'bcal', bca_p1, 1)
        dwait_total(v, 'p0c')
        v.wait_ge(sems['gps'], B_INIT)
        def cmup(i):
            cc_, hh = divmod(i, 8)
            v.wait_ge(sems['mm'], b1m + p1m_smm(i))
            v.scalar_tensor_tensor(CM[:, hh, :], CM[:, hh, :], bca_emg(hh, cc_),
                                   pbf(2 + i % 2, DV + 1),
                                   ALU.mult, ALU.add).then_inc(sems['vec'], 1)
        for i in range(8 * NCH):
            c, h = divmod(i, 8)
            v.wait_ge(sems['mm'], b1m + p1m_trk(i))
            if i >= 2:
                v.wait_ge(sems['mm'], b1m + p1m_smm(i - 2))
            v.tensor_scalar(wkb[:, i % 2, :], pbb(i % 2), RC[:, c, h:h + 1],
                            bca_gl(h, c), ALU.mult, ALU.mult).then_inc(sems['vec'], 1)
            dwait(v, 'vchl', c)
            if i >= 8:
                v.wait_ge(sems['mm'], b1m + p1m_smm(i - 8))
            v.tensor_copy(vaug[:, h, 0:DV],
                          vch[:, c % 2, h * DV:(h + 1) * DV]).then_inc(sems['vec'], 1)
            if i >= 1:
                cmup(i - 1)
        cmup(8 * NCH - 1)
    B.vector(b1_v)
    barrier()
    if "t_sum" in tapd and upto == 'B1':
        def tapsum(s):
            s.wait_ge(sems['vec'], b1v + 24 * NCH)
            dinc(s.dma_start(out=tapd["t_sum"][:, 0:2056],
                             in_=CM.ap().rearrange("p h v -> p (h v)")), 'tap', dbase('tap', 1), 1)
            dwait_total(s, 'tap')
        B.sync(tapsum)
    if upto == 'B1':
        fin_stop()
        return {}

    _sc.__exit__(None, None, None); _sc = nc.named_scope('C'); _sc.__enter__()
    c_v = reserve('vec', 1)
    def c_v1(v):
        v.tensor_tensor(m8s[:, 0:1], MT[:, T - 1:T], BT[:, T - 1:T], ALU.subtract)
        v.tensor_copy(m8s[:, 1:2], BT[:, T - 1:T]).then_inc(sems['vec'], 1)
    B.vector(c_v1)

    def c_s1(s):
        s.wait_ge(sems['vec'], c_v + 1)
        dinc(s.dma_start(out=cc_in[:, 0:2056],
                         in_=CM.ap().rearrange("p h v -> p (h v)")), 'ccst', 0, 1)
        dinc(s.dma_start(out=cc_in[:, 2056:2064], in_=CM.ap()[:, 0, 0:8]), 'ccst', 1, 1)
        dwait(s, 'ccst', 1, 1)
        dinc(s.dma_start(out=cc_in[0:8, 2056:2058], in_=m8s[:, 0:2]), 'ccst', 2, 1)
    B.sync(c_s1)

    c_cc = reserve('cc', 1)
    def c_g1(gp):
        dwait_total(gp, 'ccst')
        gp.collective_compute(
            "AllGather", ALU.bypass,
            replica_groups=[list(range(NCORES))],
            ins=[cc_in], outs=[cc_out],
        ).then_inc(sems['cc'], 1)
    B.gpsimd(c_g1)

    if "t_sum" in tapd:
        def tapf(s):
            s.wait_ge(sems['cc'], c_cc + 1)
            dinc(s.dma_start(out=tapd["t_sum"], in_=cc_in[:, :]), 'tap', dbase('tap', 1), 1)
            dwait_total(s, 'tap')
        B.sync(tapf)

    def c_sl1(s):
        s.wait_ge(sems['cc'], c_cc + 1)
        src = cc_out.rearrange("(p r) c -> r p c", r=128)[0:8, 0:7, 2056:2058]
        dinc(s.dma_start(out=GS8.ap()[0:8, 400:414].rearrange("h (p q) -> h p q", q=2),
                         in_=src), 'mbld', 0, 1)
    B.sync(c_sl1)

    NV_C1 = 1 + 7 * 10 + 7
    c_v2b = reserve('vec', NV_C1)
    c_s2b = reserve('scl', 1)

    def c_v2(v):
        dwait(v, 'mbld', 0, 1)
        k = 0
        _W = True
        def step(fn):
            nonlocal k
            fn().then_inc(sems['vec'], 1); k += 1
            if _W:
                v.wait_ge(sems['vec'], c_v2b + k)
        v.memset(m8s[:, 2:3], -1.0e9)
        v.memset(m8s[:, 3:4], 0.0)
        v.sem_inc(sems['vec'], 1); k += 1
        if _W:
            v.wait_ge(sems['vec'], c_v2b + k)
        for p in range(7):
            u = umask[:, p:p + 1]
            Mp = GS8[0:8, 400 + 2 * p:401 + 2 * p]
            Bp = GS8[0:8, 401 + 2 * p:402 + 2 * p]
            step(lambda: v.tensor_scalar(m8s[:, 4:5], Bp, u, None, ALU.mult))
            step(lambda: v.tensor_scalar(m8s[:, 5:6], Mp, u, None, ALU.mult))
            step(lambda: v.tensor_scalar(m8s[:, 6:7], u, 2.0e9, -2.0e9, ALU.mult, ALU.add))
            step(lambda: v.tensor_tensor(m8s[:, 5:6], m8s[:, 5:6], m8s[:, 6:7], ALU.add))
            step(lambda: v.tensor_tensor(m8s[:, 6:7], m8s[:, 5:6], m8s[:, 3:4], ALU.subtract))
            step(lambda: v.tensor_tensor(m8s[:, 7:8], m8s[:, 2:3], m8s[:, 6:7], ALU.max))
            step(lambda: v.tensor_tensor(GS8[0:8, 416 + p:417 + p], m8s[:, 2:3], m8s[:, 7:8], ALU.subtract))
            step(lambda: v.tensor_tensor(GS8[0:8, 424 + p:425 + p], m8s[:, 6:7], m8s[:, 7:8], ALU.subtract))
            step(lambda: v.tensor_copy(m8s[:, 2:3], m8s[:, 7:8]))
            step(lambda: v.tensor_tensor(m8s[:, 3:4], m8s[:, 3:4], m8s[:, 4:5], ALU.add))
        step(lambda: v.tensor_scalar(GS8[0:8, 423:424], umask[:, 0:1], 0.0, None, ALU.mult))
        step(lambda: v.tensor_scalar_max(m8s[:, 6:7], m8s[:, 2:3], 0.0))
        step(lambda: v.tensor_tensor(MIN8[:, 0:1], m8s[:, 3:4], m8s[:, 6:7], ALU.add))
        step(lambda: v.tensor_tensor(GS8[0:8, 431:432], m8s[:, 2:3], m8s[:, 6:7], ALU.subtract))
        step(lambda: v.tensor_scalar_max(GS8[0:8, 416:432], GS8[0:8, 416:432], -80.0))
        step(lambda: v.tensor_copy(GS8[0:8, 448:455], umask[:, :]))
        step(lambda: v.tensor_copy(GS8[0:8, 456:464], m8s[:, :]))
        assert k == NV_C1, k
    B.vector(c_v2)

    def c_s2(s):
        s.wait_ge(sems['vec'], c_v2b + NV_C1)
        s.activation(GS8[:, 432:448], GS8[:, 416:432], ACT.Exp).then_inc(sems['scl'], 1)
        s.wait_ge(sems['scl'], c_s2b + 1)
        dinc(s.dma_start(out=rows_sp2, in_=GS8[0:8, 432:448]), 'r2st', 0, 1)
    B.scalar(c_s2)

    def c_g2(gp):
        dwait(gp, 'r2st', 0, 1)
        src = rows_sp2.rearrange("h c -> (h c)")
        dinc(gp.dma_start(out=BCA[:, 0:128], in_=src[None, :].partition_broadcast(128)),
             'bcbl', 0, 1)
        cast_one(gp, 6); cast_one(gp, 7)
    B.gpsimd(c_g2)

    def bcb(h, q):
        return BCA[:, h * 16 + q: h * 16 + q + 1]

    c_v3b = reserve('vec', 7 + 2)

    def c_sy3(s):
        s.wait_ge(sems['cc'], c_cc + 1)
        for p in range(7):
            if p >= 1:
                s.wait_ge(sems['vec'], c_v3b + 1 + p)
            dinc(s.dma_start(out=gbuf[:, :],
                             in_=cc_out[p * 128:(p + 1) * 128, :]), 'gbld', p, 1)
    B.sync(c_sy3)

    def c_v3(v):
        dwait(v, 'bcbl', 0, 1)
        v.memset(CM[:, :, :], 0.0)
        v.sem_inc(sems['vec'], 1)
        for p in range(7):
            dwait(v, 'gbld', p, 1)
            for h in range(NH):
                sl = slice(h * 257, (h + 1) * 257)
                v.tensor_scalar(gbuf[:, sl], gbuf[:, sl], bcb(h, 8 + p), None, ALU.mult)
            for h in range(NH):
                sl = slice(h * 257, (h + 1) * 257)
                op = v.scalar_tensor_tensor(CM[:, h, :], CM[:, h, :], bcb(h, p),
                                            gbuf[:, sl], ALU.mult, ALU.add)
            op.then_inc(sems['vec'], 1)
        for h in range(NH):
            op = v.tensor_scalar(CM[:, h, :], CM[:, h, :], bcb(h, 15), None, ALU.mult)
        op.then_inc(sems['vec'], 1)
    B.vector(c_v3)
    barrier()
    if upto == 'C' and "t_ent" in tapd:
        def tapent(s):
            s.wait_ge(sems['vec'], c_v3b + 9)
            b = dbase('tap', 1)
            dinc(s.dma_start(out=tapd["t_ent"], in_=CM.ap().rearrange("p h v -> p (h v)")), 'tap', b, 1)
            dinc(s.dma_start(out=tapd["t_min"], in_=MIN8[:, :]), 'tap', b + 1, 1)
            if "t_gs8" in tapd:
                dinc(s.dma_start(out=tapd["t_gs8"], in_=GS8[0:8, :]), 'tap', b + 2, 1)
            if "t_bca" in tapd:
                dinc(s.dma_start(out=tapd["t_bca"], in_=BCA[:, 0:128]), 'tap', b + 3, 1)
            dwait_total(s, 'tap')
        B.sync(tapent)
    if upto == 'C':
        fin_stop()
        return {}
    if "t_ent" in tapd:
        def tapf(s):
            b = dbase('tap', 1)
            dinc(s.dma_start(out=tapd["t_ent"], in_=CM.ap().rearrange("p h v -> p (h v)")), 'tap', b, 1)
            dinc(s.dma_start(out=tapd["t_min"], in_=MIN8[:, :]), 'tap', b + 1, 1)
            dwait_total(s, 'tap')
        B.sync(tapf)
        barrier()

    _sc.__exit__(None, None, None); _sc = nc.named_scope('B2'); _sc.__enter__()
    bca2 = emit_prep(True)
    vchb = dbase('vchl')

    def b2p_s(s):
        dinc(s.dma_start(out=qTs[:, :, :],
                         in_=q_sp.rearrange("h p t -> p h t")), 'qtl', 0, 1)
        dinc(s.dma_start(out=vch[:, 0, :], in_=v_sp[0:128, :]), 'vchl', vchb)
        dinc(s.dma_start(out=sogch[:, 0, :], in_=sog_sp[0:128, :]), 'sgl', 0)
    B.sync(b2p_s)

    def b2p_g(gp):
        for h in range(NH):
            dinc(gp.dma_start(out=gamma_bc[:, h * DV:(h + 1) * DV],
                              in_=gamma_i[h:h + 1, :].partition_broadcast(128)), 'gml', h, 1)
        for j in range(8, 12):
            cast_one(gp, j)
    B.gpsimd(b2p_g)

    barrier()

    NCHB = NCH
    M2, V2, S2, G2 = 56, 100, 28, 16
    def q_pm_S(h): return 12 * (h // 4) + 3 * (h % 4) + 1
    def q_pm_P1(h): return 12 * (h // 4) + 3 * (h % 4) + 2
    def q_pm_P2(h): return 12 * (h // 4) + 3 * (h % 4) + 3
    def q_pm_TRK(h): return 24 + 2 * h + 1
    def q_pm_SMM(h): return 24 + 2 * h + 2
    def q_pm_TRH(a): return 40 + a + 1
    def q_pv_C0A(h): return h + 1
    def q_pv_wst(h): return 8 + 18 * (h // 4) + 2 * (h % 4) + 1
    def q_pv_va(h): return 8 + 18 * (h // 4) + 2 * (h % 4) + 2
    def q_pv_gath(j, g2): return 8 + 18 * g2 + 8 + j + 1
    def q_pv_chain(k, g2): return 8 + 18 * g2 + 12 + k
    def q_pv_wk(h): return 44 + 2 * h + 1
    def q_pv_cm(h): return 44 + 2 * h + 2
    def q_pv_bns(h): return 60 + 3 * h + 1
    def q_pv_bna(h): return 60 + 3 * h + 2
    def q_pv_hm(h): return 60 + 3 * h + 3
    def q_pv_trhev(a): return 84 + a + 1
    def q_ps_chain(k, g2): return 6 * g2 + k
    def q_ps_hev(h): return 6 * (h // 4) + 2 + (h % 4) + 1
    def q_ps_rsln(h): return 12 + 2 * h + 1
    def q_ps_rsex(h): return 12 + 2 * h + 2
    def q_pg_gam(h): return 2 * h + 1
    def q_pg_sog(h): return 2 * h + 2

    b2m = reserve('mm', M2 * NCHB)
    b2v = reserve('vec', V2 * NCHB)
    b2s = reserve('scl', S2 * NCHB)
    b2g = reserve('gps', G2 * NCHB)

    def b2_sync(s):
        def hstore(c):
            hf, cl = divmod(c, 8)
            s.wait_ge(sems['vec'], b2v + V2 * c + q_pv_trhev(15))
            dinc(s.dma_start(out=hout_sp[hf, :, :, cl * 128:(cl + 1) * 128],
                             in_=houtTc[:, :, :]), 'hst', c, 1)
        for cn in range(1, NCHB):
            if cn >= 2:
                s.wait_ge(sems['vec'], b2v + V2 * (cn - 2) + q_pv_va(7))
            dinc(s.dma_start(out=vch[:, cn % 2, :],
                             in_=v_sp[cn * 128:(cn + 1) * 128, :]), 'vchl', vchb + cn)
            if cn >= 2:
                s.wait_ge(sems['gps'], b2g + G2 * (cn - 2) + q_pg_sog(7))
            dinc(s.dma_start(out=sogch[:, cn % 2, :],
                             in_=sog_sp[cn * 128:(cn + 1) * 128, :]), 'sgl', cn)
            hstore(cn - 1)
        hstore(NCHB - 1)
    B.sync(b2_sync)

    def b2_t(t):
        dwait(t, 'qtl', 0, 1)
        for c in range(NCHB):
            for g2 in range(2):
                for j in range(4):
                    h = 4 * g2 + j
                    if h == 0 and c >= 1:
                        t.wait_ge(sems['vec'], b2v + V2 * (c - 1) + V2)
                    if h >= 2:
                        t.wait_ge(sems['vec'], b2v + V2 * c + q_pv_wst(h - 2))
                    t.matmul(pbf(h % 2, 128), kT[:, h, c * 128:(c + 1) * 128],
                             qTs[:, h, c * 128:(c + 1) * 128],
                             start=True, stop=True).then_inc(sems['mm'], 1)
                    t.wait_ge(sems['vec'], b2v + V2 * c + q_pv_va(h))
                    if h >= 4:
                        t.wait_ge(sems['scl'], b2s + S2 * c + q_ps_hev(h - 4))
                        t.wait_ge(sems['vec'], b2v + V2 * c + q_pv_gath(h - 4, 0))
                    elif c >= 1:
                        t.wait_ge(sems['scl'], b2s + S2 * (c - 1) + q_ps_hev(h + 4))
                        t.wait_ge(sems['vec'], b2v + V2 * (c - 1) + q_pv_gath(h, 1))
                    t.matmul(pbf(4 + j, DV + 1), wstb[:, h % 2, :], vaug[:, h, :],
                             start=True, stop=False).then_inc(sems['mm'], 1)
                    t.wait_ge(sems['vec'], b2v + V2 * c + q_pv_C0A(h))
                    t.matmul(pbf(4 + j, DV + 1), qTs[:, h, c * 128:(c + 1) * 128],
                             C0A[:, h, :],
                             start=False, stop=True).then_inc(sems['mm'], 1)
            for h in range(NH):
                if h >= 1:
                    t.wait_ge(sems['vec'], b2v + V2 * c + q_pv_wk(h - 1))
                elif c >= 1:
                    t.wait_ge(sems['vec'], b2v + V2 * (c - 1) + q_pv_wk(7))
                t.transpose(pbb(2), kT[:, h, c * 128:(c + 1) * 128],
                            ident[:, :]).then_inc(sems['mm'], 1)
                t.wait_ge(sems['vec'], b2v + V2 * c + q_pv_wk(h))
                if h >= 1:
                    t.wait_ge(sems['vec'], b2v + V2 * c + q_pv_cm(h - 1))
                elif c >= 1:
                    t.wait_ge(sems['vec'], b2v + V2 * (c - 1) + q_pv_cm(7))
                t.matmul(pbf(3, DV + 1), wkb[:, h % 2, :], vaug[:, h, :],
                         start=True, stop=True).then_inc(sems['mm'], 1)
            for a in range(16):
                if a == 0:
                    t.wait_ge(sems['gps'], b2g + G2 * c + q_pg_sog(7))
                if a >= 2:
                    t.wait_ge(sems['vec'], b2v + V2 * c + q_pv_trhev(a - 2))
                t.transpose(pbb(a % 2), houtst[:, a * 128:(a + 1) * 128],
                            ident[:, :]).then_inc(sems['mm'], 1)
    B.tensor(b2_t)

    def b2_v(v):
        dwait(v, 'bcal', bca2, 1)
        dwait(v, 'gml', NH - 1, 1)
        for c in range(NCHB):
            kv = 0
            for h in range(NH):
                if c >= 1:
                    v.wait_ge(sems['mm'], b2m + M2 * (c - 1) + q_pm_P2(h))
                v.tensor_scalar(C0A[:, h, :], CM[:, h, :], bca_em0(h, c), None,
                                ALU.mult).then_inc(sems['vec'], 1); kv += 1
            for g2 in range(2):
                for j in range(4):
                    h = 4 * g2 + j
                    v.wait_ge(sems['mm'], b2m + M2 * c + q_pm_S(h))
                    if h >= 2:
                        v.wait_ge(sems['mm'], b2m + M2 * c + q_pm_P1(h - 2))
                    elif c >= 1:
                        v.wait_ge(sems['mm'], b2m + M2 * (c - 1) + q_pm_P1(h + 6))
                    v.scalar_tensor_tensor(wstb[:, h % 2, :], pbf(h % 2, 128),
                                           RC[:, c, h:h + 1], tri[:, :],
                                           ALU.mult, ALU.mult).then_inc(sems['vec'], 1); kv += 1
                    dwait(v, 'vchl', vchb + c)
                    if c >= 1:
                        v.wait_ge(sems['mm'], b2m + M2 * (c - 1) + q_pm_SMM(h))
                    v.tensor_copy(vaug[:, h, 0:DV],
                                  vch[:, c % 2, h * DV:(h + 1) * DV]).then_inc(sems['vec'], 1); kv += 1
                for j in range(4):
                    h = 4 * g2 + j
                    v.wait_ge(sems['mm'], b2m + M2 * c + q_pm_P2(h))
                    v.tensor_copy(rec4[:, 12 + j:13 + j],
                                  pbf(4 + j, DV + 1)[:, DV:DV + 1]).then_inc(sems['vec'], 1); kv += 1
                v.wait_ge(sems['vec'], b2v + V2 * c + q_pv_gath(3, g2))
                v.tensor_tensor(rec4[:, 8:12], rec4[:, 12:16], GALL[:, c, 4 * g2:4 * g2 + 4],
                                ALU.mult).then_inc(sems['vec'], 1); kv += 1
                v.wait_ge(sems['vec'], b2v + V2 * c + q_pv_chain(1, g2))
                v.tensor_scalar_mul(rec4[:, 12:16], rec4[:, 8:12], -1.0).then_inc(sems['vec'], 1); kv += 1
                v.wait_ge(sems['vec'], b2v + V2 * c + q_pv_chain(2, g2))
                v.tensor_tensor(rec4[:, 8:12], rec4[:, 8:12], rec4[:, 12:16], ALU.max).then_inc(sems['vec'], 1); kv += 1
                v.wait_ge(sems['vec'], b2v + V2 * c + q_pv_chain(3, g2))
                v.tensor_tensor(rec4[:, 8:12], rec4[:, 8:12], EMALL[:, c, 4 * g2:4 * g2 + 4],
                                ALU.max).then_inc(sems['vec'], 1); kv += 1
                v.wait_ge(sems['vec'], b2v + V2 * c + q_pv_chain(4, g2))
                v.tensor_scalar_add(rec4[:, 8:12], rec4[:, 8:12], MH_EPS).then_inc(sems['vec'], 1); kv += 1
                v.wait_ge(sems['vec'], b2v + V2 * c + q_pv_chain(5, g2))
                v.wait_ge(sems['scl'], b2s + S2 * c + q_ps_chain(2, g2))
                v.tensor_tensor(rec4[:, 4 * g2:4 * g2 + 4], rec4[:, 8:12],
                                GALL[:, c, 4 * g2:4 * g2 + 4], ALU.mult).then_inc(sems['vec'], 1); kv += 1
            for h in range(NH):
                v.wait_ge(sems['mm'], b2m + M2 * c + q_pm_TRK(h))
                if h >= 2:
                    v.wait_ge(sems['mm'], b2m + M2 * c + q_pm_SMM(h - 2))
                elif c >= 1:
                    v.wait_ge(sems['mm'], b2m + M2 * (c - 1) + q_pm_SMM(h + 6))
                v.tensor_scalar(wkb[:, h % 2, :], pbb(2), RC[:, c, h:h + 1],
                                bca_gl(h, c), ALU.mult, ALU.mult).then_inc(sems['vec'], 1); kv += 1
                v.wait_ge(sems['mm'], b2m + M2 * c + q_pm_SMM(h))
                v.scalar_tensor_tensor(CM[:, h, :], CM[:, h, :], bca_emg(h, c),
                                       pbf(3, DV + 1), ALU.mult, ALU.add).then_inc(sems['vec'], 1); kv += 1
            for h in range(NH):
                v.wait_ge(sems['scl'], b2s + S2 * c + q_ps_hev(h))
                v.bn_stats(stat6[:, 0:6], h_raw[:, h, :]).then_inc(sems['vec'], 1); kv += 1
                v.wait_ge(sems['vec'], b2v + V2 * c + q_pv_bns(h))
                if h >= 1:
                    v.wait_ge(sems['scl'], b2s + S2 * c + q_ps_rsln(h - 1))
                v.bn_aggr(stat6[:, 6:8], stat6[:, 0:6]).then_inc(sems['vec'], 1); kv += 1
                v.wait_ge(sems['vec'], b2v + V2 * c + q_pv_bna(h))
                v.wait_ge(sems['scl'], b2s + S2 * c + q_ps_rsex(h))
                if h >= 2:
                    v.wait_ge(sems['gps'], b2g + G2 * c + q_pg_gam(h - 2))
                elif c >= 1:
                    v.wait_ge(sems['gps'], b2g + G2 * (c - 1) + q_pg_gam(h + 6))
                v.tensor_scalar(hmb[:, h % 2, :], h_raw[:, h, :], stat6[:, 6:7],
                                srows[:, 2 + h % 2:3 + h % 2], ALU.subtract,
                                ALU.mult).then_inc(sems['vec'], 1); kv += 1
            for a in range(16):
                if a == 0 and c >= 1:
                    dwait(v, 'hst', c - 1, 1)
                v.wait_ge(sems['mm'], b2m + M2 * c + q_pm_TRH(a))
                v.tensor_copy(houtTc[:, a, :], pbb(a % 2)).then_inc(sems['vec'], 1); kv += 1
            assert kv == V2, kv
    B.vector(b2_v)

    def b2_s(s):
        for c in range(NCHB):
            ks = 0
            for g2 in range(2):
                s.wait_ge(sems['vec'], b2v + V2 * c + q_pv_chain(5, g2))
                s.activation(rec4[:, 12:16], rec4[:, 8:12], ACT.Ln).then_inc(sems['scl'], 1); ks += 1
                s.wait_ge(sems['scl'], b2s + S2 * c + q_ps_chain(1, g2))
                s.activation(rec4[:, 8:12], rec4[:, 12:16], ACT.Exp, scale=-1.0).then_inc(sems['scl'], 1); ks += 1
                for j in range(4):
                    h = 4 * g2 + j
                    s.wait_ge(sems['vec'], b2v + V2 * c + q_pv_chain(6, g2))
                    if c >= 1:
                        s.wait_ge(sems['vec'], b2v + V2 * (c - 1) + q_pv_hm(h))
                    s.activation(h_raw[:, h, :], pbf(4 + j, DV + 1)[:, 0:DV],
                                 ACT.Copy, scale=rec4[:, h:h + 1]).then_inc(sems['scl'], 1); ks += 1
            for h in range(NH):
                s.wait_ge(sems['vec'], b2v + V2 * c + q_pv_bna(h))
                if h >= 2:
                    s.wait_ge(sems['vec'], b2v + V2 * c + q_pv_hm(h - 2))
                s.activation(srows[:, 0:1], stat6[:, 7:8], ACT.Ln, bias=MH_EPS).then_inc(sems['scl'], 1); ks += 1
                s.wait_ge(sems['scl'], b2s + S2 * c + q_ps_rsln(h))
                s.activation(srows[:, 2 + h % 2:3 + h % 2], srows[:, 0:1],
                             ACT.Exp, scale=-0.5).then_inc(sems['scl'], 1); ks += 1
            assert ks == S2, ks
    B.scalar(b2_s)

    def b2_g(gp):
        for c in range(NCHB):
            for h in range(NH):
                gp.wait_ge(sems['vec'], b2v + V2 * c + q_pv_hm(h))
                if h == 0 and c >= 1:
                    gp.wait_ge(sems['mm'], b2m + M2 * c)
                gp.tensor_tensor(hgb[:, h % 2, :], hmb[:, h % 2, :],
                                 gamma_bc[:, h * DV:(h + 1) * DV], ALU.mult).then_inc(sems['gps'], 1)
                dwait(gp, 'sgl', c)
                gp.tensor_tensor(houtst[:, h * DV:(h + 1) * DV], hgb[:, h % 2, :],
                                 sogch[:, c % 2, h * DV:(h + 1) * DV], ALU.mult).then_inc(sems['gps'], 1)
            if 12 + c < 28:
                cast_one(gp, 12 + c)
    B.gpsimd(b2_g)
    barrier()
    if upto == 'B2':
        if "t_ho" in tapd:
            def tapho(s):
                b = dbase('tap', 1)
                dinc(s.dma_start(out=tapd["t_ho"],
                                 in_=hout_sp.rearrange("f p a n -> (f p) (a n)")),
                     'tap', b, 1)
                dwait_total(s, 'tap')
            B.sync(tapho)
        fin_stop()
        return {}

    _sc.__exit__(None, None, None); _sc = nc.named_scope('WL2'); _sc.__enter__()
    for hf in range(2):
        def w_hload(s, hf=hf):
            dwait_total(s, 'hst')
            dinc(s.dma_start(out=houtW[:, :, :], in_=hout_sp[hf]), 'hwl', hf, 1)
        B.sync(w_hload)

        def w_plan(add, hf=hf):
            def dma_w(it):
                def fn(e, pos):
                    if it >= 2:
                        e.wait_ge(sems['mm'], pos('tensor', ('t', it - 1, 15)))
                    dinc(e.dma_start(out=wstW[:, it % 2, :],
                                     in_=Wout_i[it * 128:(it + 1) * 128, :]), 'wold', hf * 16 + it)
                    return None
                return fn

            def tr(it, a):
                def fn(t, pos):
                    if a == 0:
                        dwait(t, 'wold', hf * 16 + it)
                        if it >= 2:
                            t.wait_ge(sems['vec'], pos('vector', ('w', it - 2, 15)))
                    return t.transpose(pbt(it % 2, a), wstW[:, it % 2, a * 128:(a + 1) * 128],
                                       ident[:, :])
                return fn

            def wev(it, a):
                def fn(v, pos):
                    v.wait_ge(sems['mm'], pos('tensor', ('t', it, 7 if a < 8 else 15)))
                    vb, rr = divmod(it, 4)
                    return v.tensor_copy(WTw(vb % 2, a)[:, rr * 128:(rr + 1) * 128],
                                         pbt(it % 2, a))
                return fn

            def grp(ob, ttl):
                def fn(t, pos):
                    idx = ob * 8 + ttl
                    if idx == 0:
                        dwait(t, 'hwl', hf, 1)
                    if ttl == 0:
                        t.wait_ge(sems['vec'], pos('vector', ('w', ob * 4 + 3, 15)))
                    if idx >= 4:
                        t.wait_ge(sems['vec'], pos('vector', ('g', (idx - 4) // 8, (idx - 4) % 8)))
                    last = None
                    for a in range(16):
                        last = t.matmul(pbf(4 + idx % 4), houtW[:, a, ttl * 128:(ttl + 1) * 128],
                                        WTw(ob % 2, a),
                                        start=(a == 0), stop=(a == 15))
                    return last
                return fn

            def gev(ob, ttl):
                def fn(v, pos):
                    idx = ob * 8 + ttl
                    v.wait_ge(sems['mm'], pos('tensor', ('g', ob, ttl)))
                    dwait(v, 'xpst', hf * 32 + idx - 2)
                    return v.tensor_copy(ystW[:, idx % 2, :], pbf(4 + idx % 4))
                return fn

            for b in range(5):
                if b >= 1:
                    for ttl in range(8):
                        add('tensor', ('g', b - 1, ttl), grp(b - 1, ttl))
                if b < 4:
                    for rr in range(4):
                        it = b * 4 + rr
                        for a in range(16):
                            add('tensor', ('t', it, a), tr(it, a))
                if b >= 1:
                    for rr in range(4):
                        it = (b - 1) * 4 + rr
                        for a in range(16):
                            add('vector', ('w', it, a), wev(it, a))
                    for ttl in range(8):
                        add('vector', ('g', b - 1, ttl), gev(b - 1, ttl))
            for it in range(16):
                add('gpsimd', ('l', it), dma_w(it))
        w_pos = run_pipe(w_plan)

        def w_store(s, w_pos=w_pos, hf=hf):
            for ob in range(4):
                for ttl in range(8):
                    idx = ob * 8 + ttl
                    tt = hf * 8 + ttl
                    s.wait_ge(sems['vec'], w_pos('vector', ('g', ob, ttl)))
                    dinc(s.dma_start(out=xmid_sp[tt * 128:(tt + 1) * 128, ob * 512:(ob + 1) * 512],
                                     in_=ystW[:, idx % 2, :]), 'xpst', hf * 32 + idx)
        B.scalar(w_store)
        barrier()

        def l2_plan(add, hf=hf):
            def dma_ya(ttl):
                def fn(e, pos):
                    tt = hf * 8 + ttl
                    if ttl == 0:
                        dwait_total(e, 'xpst')
                    if ttl >= 2:
                        e.wait_ge(sems['vec'], pos('vector', ('n', ttl - 2)))
                    dinc(e.dma_start(out=xaW[:, ttl % 2, :],
                                     in_=xmid_sp[tt * 128:(tt + 1) * 128, :]), 'yal', hf * 8 + ttl)
                    return None
                return fn

            def dma_x(ttl):
                def fn(e, pos):
                    tt = hf * 8 + ttl
                    if ttl >= 2:
                        e.wait_ge(sems['vec'], pos('vector', ('a', ttl - 2)))
                    dinc(e.dma_start(out=xbW[:, ttl % 2, :],
                                     in_=x_in[tt * 128:(tt + 1) * 128, :]), 'xbl', hf * 8 + ttl)
                    return None
                return fn

            def addop(ttl):
                def fn(v, pos):
                    dwait(v, 'yal', hf * 8 + ttl)
                    dwait(v, 'xbl', hf * 8 + ttl)
                    dwait(v, 'xms', hf * 8 + ttl - 2)
                    return v.tensor_tensor(xaW[:, ttl % 2, :], xaW[:, ttl % 2, :],
                                           xbW[:, ttl % 2, :], ALU.add)
                return fn

            def statsp(ttl):
                def fn(v, pos):
                    v.wait_ge(sems['vec'], pos('vector', ('a', ttl)))
                    last = None
                    for q4 in range(4):
                        last = v.bn_stats(stat6[:, 6 * q4:6 * q4 + 6], xaW[:, ttl % 2, q4 * 512:(q4 + 1) * 512])
                    return last
                return fn

            def stats(ttl):
                def fn(v, pos):
                    v.wait_ge(sems['vec'], pos('vector', ('sp', ttl)))
                    return v.bn_aggr(stat6[:, 24:26], stat6[:, 0:24])
                return fn

            def rstd1(ttl):
                def fn(s, pos):
                    s.wait_ge(sems['vec'], pos('vector', ('s', ttl)))
                    return s.activation(srows[:, 1:2], stat6[:, 25:26], ACT.Ln, bias=LN_EPS)
                return fn

            def rstd(ttl):
                def fn(s, pos):
                    s.wait_ge(sems['scl'], pos('scalar', ('r1', ttl)))
                    return s.activation(srows[:, 0:1], srows[:, 1:2], ACT.Exp, scale=-0.5)
                return fn

            def xm_store(ttl):
                def fn(s, pos):
                    tt = hf * 8 + ttl
                    s.wait_ge(sems['vec'], pos('vector', ('a', ttl)))
                    dinc(s.dma_start(out=xmid_sp[tt * 128:(tt + 1) * 128, :],
                                     in_=xaW[:, ttl % 2, :]), 'xms', hf * 8 + ttl)
                    return None
                return fn

            def norm(ttl):
                def fn(v, pos):
                    v.wait_ge(sems['scl'], pos('scalar', ('r', ttl)))
                    if ttl >= 2:
                        v.wait_ge(sems['mm'], pos('tensor', ('t', ttl - 2, 15)))
                    return v.tensor_scalar(x2st[:, ttl % 2, :], xaW[:, ttl % 2, :],
                                           stat6[:, 24:25], srows[:, 0:1],
                                           ALU.subtract, ALU.mult)
                return fn

            def tr(ttl, a):
                def fn(t, pos):
                    if a == 0:
                        t.wait_ge(sems['vec'], pos('vector', ('n', ttl)))
                        if ttl >= 2:
                            t.wait_ge(sems['vec'], pos('vector', ('e', ttl - 2, 15)))
                    return t.transpose(pbt(ttl % 2, a), x2st[:, ttl % 2, a * 128:(a + 1) * 128],
                                       ident[:, :])
                return fn

            def ev(ttl, a):
                def fn(v, pos):
                    v.wait_ge(sems['mm'], pos('tensor', ('t', ttl, 7 if a < 8 else 15)))
                    return v.tensor_copy(x2T[:, a, ttl * 128:(ttl + 1) * 128], pbt(ttl % 2, a))
                return fn

            for ttl in range(8):
                add('sync', ('ya', ttl), dma_ya(ttl))
                add('sync', ('xb', ttl), dma_x(ttl))
                add('vector', ('a', ttl), addop(ttl))
                add('vector', ('sp', ttl), statsp(ttl))
                add('vector', ('s', ttl), stats(ttl))
                add('scalar', ('r1', ttl), rstd1(ttl))
                add('scalar', ('r', ttl), rstd(ttl))
                add('scalar', ('x', ttl), xm_store(ttl))
                add('vector', ('n', ttl), norm(ttl))
                if ttl >= 1:
                    for a in range(16):
                        add('vector', ('e', ttl - 1, a), ev(ttl - 1, a))
                for a in range(16):
                    add('tensor', ('t', ttl, a), tr(ttl, a))
            for a in range(16):
                add('vector', ('e', 7, a), ev(7, a))
        l2_pos = run_pipe(l2_plan)

        def l_out_f(s, l2_pos=l2_pos, hf=hf):
            s.wait_ge(sems['vec'], l2_pos('vector', ('e', 7, 15)))
            dinc(s.dma_start(out=xlt_sp[hf], in_=x2T[:, :, :]), 'xlst', hf, 1)
            dwait(s, 'xlst', hf, 1)
        B.sync(l_out_f)
        barrier()
        if upto == 'W' and hf == 1:
            fin_stop()
            return {}

    _sc.__exit__(None, None, None); _sc = nc.named_scope('F'); _sc.__enter__()
    def f_pre(gp):
        dinc(gp.dma_start(out=gsus[:, 0:48],
                          in_=gate_si.rearrange("o (a p) -> p (o a)", p=128),
                          allow_slow_non_contiguous=True), 'fsc', 0, 1)
        dinc(gp.dma_start(out=gsus[:, 48:96],
                          in_=up_si.rearrange("o (a p) -> p (o a)", p=128),
                          allow_slow_non_contiguous=True), 'fsc', 1, 1)
    B.gpsimd(f_pre)

    for hf in range(2):
        def f_ld(s, hf=hf):
            dinc(s.dma_start(out=x2Tf[:, :, :], in_=xlt_sp[hf]), 'xtf', hf, 1)
        B.sync(f_ld)

        NGRP = 12 * 4 * 2
        f1_mm = reserve('mm', 12 * 16)
        f1_sc = reserve('scl', NGRP)
        f1_vu = reserve('vec', NGRP)
        f1_gp = reserve('gps', NGRP)

        def f1_g(g, f1_mm=f1_mm, f1_sc=f1_sc, f1_vu=f1_vu, hf=hf):
            dwait_total(g, 'dqst')
            def load(nb):
                for wi in range(2):
                    if nb >= 2:
                        g.wait_ge(sems['mm'], f1_mm + (nb - 2) * 16 + 15 + wi)
                    dst = (g8r if wi == 0 else u8r).ap()[:, (nb % 2) * 8192:(nb % 2 + 1) * 8192]
                    dinc(g.dma_start(out=dst,
                                     in_=gw8_sp[wi * 12 + nb]
                                     ), 'ggl' if wi == 0 else 'gul', hf * 12 + nb)
            def fuse_grp(nb):
                for nt4 in range(4):
                    for tb in range(2):
                        k = nb * 8 + nt4 * 2 + tb
                        nt = nb * 4 + nt4
                        g.wait_ge(sems['scl'], f1_sc + k + 1)
                        g.wait_ge(sems['vec'], f1_vu + k + 1)
                        g.tensor_tensor(fuse[:, nt, tb * 512:(tb + 1) * 512],
                                        gub[:, k % 2, :], uub[:, k % 2, :],
                                        ALU.mult).then_inc(sems['gps'], 1)
            load(0); load(1)
            for nb in range(2, 12):
                fuse_grp(nb - 2)
                load(nb)
            fuse_grp(10); fuse_grp(11)
        B.gpsimd(f1_g)

        def f2_gld(gp, hf=hf):
            dinc(gp.dma_start(out=ds_bc[:, :],
                              in_=down_si.partition_broadcast(128)), 'fsc', 2 + hf, 1)
        _F2GLD = f2_gld

        def f1_t(t, f1_sc=f1_sc, f1_vu=f1_vu, hf=hf):
            dwait(t, 'xtf', hf, 1)
            for nb in range(12):
                dwait(t, 'ggl', hf * 12 + nb)
                dwait(t, 'gul', hf * 12 + nb)
                for nt4 in range(4):
                    for tb in range(2):
                        k = nb * 8 + nt4 * 2 + tb
                        for wi in range(2):
                            srcf = (g8f if wi == 0 else u8f)
                            bank = (2 if wi == 0 else 4) + (k % 2)
                            if k >= 2:
                                if wi == 0:
                                    t.wait_ge(sems['scl'], f1_sc + k - 1)
                                else:
                                    t.wait_ge(sems['vec'], f1_vu + k - 1)
                            for a in range(16):
                                mmi = t.matmul(pbf(bank), srcf(nb % 2, a)[:, nt4 * 128:(nt4 + 1) * 128],
                                               x2Tf[:, a, tb * 512:(tb + 1) * 512],
                                               start=(a == 0), stop=(a == 15))
                                if a == 15:
                                    mmi.then_inc(sems['mm'], 1)
        B.tensor(f1_t)

        def f1_s(s, f1_mm=f1_mm, f1_gp=f1_gp):
            dwait(s, 'fsc', 0, 1)
            for nb in range(12):
                for nt4 in range(4):
                    for tb in range(2):
                        k = nb * 8 + nt4 * 2 + tb
                        s.wait_ge(sems['mm'], f1_mm + nb * 16 + (nt4 * 2 + tb) * 2 + 1)
                        if k >= 2:
                            s.wait_ge(sems['gps'], f1_gp + k - 1)
                        nt = nb * 4 + nt4
                        s.activation(gub[:, k % 2, :], pbf(2 + (k % 2)),
                                     ACT.Silu, scale=gsus[:, nt:nt + 1]).then_inc(sems['scl'], 1)
        B.scalar(f1_s)

        def f1_v(v, f1_mm=f1_mm, f1_gp=f1_gp):
            dwait(v, 'fsc', 1, 1)
            for nb in range(12):
                for nt4 in range(4):
                    for tb in range(2):
                        k = nb * 8 + nt4 * 2 + tb
                        v.wait_ge(sems['mm'], f1_mm + nb * 16 + (nt4 * 2 + tb) * 2 + 2)
                        if k >= 2:
                            v.wait_ge(sems['gps'], f1_gp + k - 1)
                        nt = nb * 4 + nt4
                        v.tensor_scalar(uub[:, k % 2, :], pbf(4 + (k % 2)),
                                        gsus[:, 48 + nt:49 + nt], None, ALU.mult).then_inc(sems['vec'], 1)
        B.vector(f1_v)

        barrier()
        B.gpsimd(_F2GLD)

        f2_mm = reserve('mm', 4 * NDT)
        f2_ev = reserve('vec', 4 * 8 * 2)

        def f2_sp(s, f2_mm=f2_mm, f2_ev=f2_ev, hf=hf):
            dwait_total(s, 'xms')
            dwait_total(s, 'dqst')
            def xaf_load(ttl):
                tt = hf * 8 + ttl
                if ttl >= 2:
                    prev = ttl - 2
                    gidx = (prev // 2) * 8 + (prev % 2) * 4 + 3
                    s.wait_ge(sems['vec'], f2_ev + (gidx + 1) * 2)
                dinc(s.dma_start(out=xaF[:, ttl % 2, :],
                                 in_=xmid_sp[tt * 128:(tt + 1) * 128, :]), 'xaf', hf * 8 + ttl)
            xaf_load(0); xaf_load(1)
            for tg in range(4):
                for nt in range(NDT):
                    li = hf * 4 * NDT + tg * NDT + nt
                    if tg * NDT + nt >= 4:
                        s.wait_ge(sems['mm'], f2_mm + tg * NDT + nt - 3)
                    dinc(s.dma_start(out=dn8[:, nt % 4, :],
                                     in_=down8_sp[nt]), 'dnl', li, 4)
                for g2i in range(8):
                    gidx = tg * 8 + g2i
                    t2, ob = g2i // 4, g2i % 4
                    tt2 = hf * 8 + tg * 2 + t2
                    s.wait_ge(sems['vec'], f2_ev + tg * 16 + t2 * 8 + (ob // 2) * 4 + 3 + (ob % 2))
                    dinc(s.dma_start(out=y_out[tt2 * 128:(tt2 + 1) * 128, ob * 512:(ob + 1) * 512],
                                     in_=ystF[:, gidx % 2, :]), 'yst', hf * 32 + gidx)
                if tg < 3:
                    xaf_load(2 * tg + 2); xaf_load(2 * tg + 3)
        B.sync(f2_sp)

        def f2_t(t, f2_ev=f2_ev, hf=hf):
            for tg in range(4):
                if tg >= 1:
                    t.wait_ge(sems['vec'], f2_ev + tg * 16)
                for nt in range(NDT):
                    dwait(t, 'dnl', hf * 4 * NDT + tg * NDT + nt, 4)
                    last = None
                    for t2 in range(2):
                        ttl = tg * 2 + t2
                        for ob in range(4):
                            bank = t2 * 4 + ob
                            last = t.matmul(pbf(bank), fuse[:, nt, ttl * 128:(ttl + 1) * 128],
                                            dn8[:, nt % 4, ob * 512:(ob + 1) * 512],
                                            start=(nt == 0), stop=(nt == NDT - 1))
                    last.then_inc(sems['mm'], 1)
        B.tensor(f2_t)

        def f2_v(v, f2_mm=f2_mm, hf=hf):
            dwait(v, 'fsc', 2 + hf, 1)
            for tg in range(4):
                v.wait_ge(sems['mm'], f2_mm + tg * NDT + NDT)
                for t2 in range(2):
                    ttl = tg * 2 + t2
                    dwait(v, 'xaf', hf * 8 + ttl)
                    for op in range(2):
                        gidx = tg * 8 + t2 * 4 + 2 * op
                        for gx in (gidx, gidx + 1):
                            ob = gx - tg * 8 - t2 * 4
                            if hf * 32 + gx - 2 >= 0:
                                dwait(v, 'yst', hf * 32 + gx - 2)
                            v.tensor_tensor(ystF[:, gx % 2, :], pbf(t2 * 4 + ob),
                                            ds_bc[:, ob * 512:(ob + 1) * 512],
                                            ALU.mult).then_inc(sems['vec'], 1)
                        for gx in (gidx, gidx + 1):
                            ob = gx - tg * 8 - t2 * 4
                            v.tensor_tensor(ystF[:, gx % 2, :], ystF[:, gx % 2, :],
                                            xaF[:, ttl % 2, ob * 512:(ob + 1) * 512],
                                            ALU.add).then_inc(sems['vec'], 1)
        B.vector(f2_v)

        barrier()

    _sc.__exit__(None, None, None)

    def fin(s):
        for nm in ('yst', 'tap', 'xms', 'hst', 'xpst', 'xlst'):
            dwait_total(s, nm)
    B.sync(fin)
    return {}


_CACHED = None


def _get_nc():
    global _CACHED
    if _CACHED is None:
        _CACHED = build()
    return _CACHED


def kernel(**inputs):
    nc = _get_nc()
    in_maps = make_in_maps(inputs)
    res = run_bass_kernel_spmd(nc, in_maps, core_ids=list(range(8)))
    out = np.concatenate([np.asarray(res.results[c]["y"]) for c in range(8)], axis=0)
    return out.reshape(1, 16384, 2048).astype(np.float32, copy=False)
